# version 15
# baseline (speedup 1.0000x reference)
import contextlib
import numpy as np

import concourse.bass as bass
import concourse.mybir as mybir
from concourse.bass_utils import run_bass_kernel_spmd

F32 = mybir.dt.float32
BF16 = mybir.dt.bfloat16
AF = mybir.ActivationFunctionType
OP = mybir.AluOpType
AX = mybir.AxisListType

T = 1024
NTP = T // 128
E = 2048
KT = E // 128
DFF = 8192
NE = 4
H = 16
D = 128
M = 1024
NMT = M // 128
TH = 512
FC = 128
NCH = DFF // FC
FG = 2048
NKF = FG // 128
NGRP = DFF // FG
DCH = 512
NDCH = E // DCH


def _vg(v):
    v.drain()


class Sem:

    def __init__(self, h):
        self.h = h
        self.n = 0

    def inc(self, ins, k=1):
        ins.then_inc(self.h, k)
        self.n += k
        return self.n


def build_program():
    nc = bass.Bass("TRN2", target_bir_lowering=False)

    x_d = nc.dram_tensor("x", [T, E], F32, kind="ExternalInput").ap()
    win_d = nc.dram_tensor("w_in", [E, E], F32, kind="ExternalInput").ap()
    wq_d = nc.dram_tensor("w_q", [D, D], F32, kind="ExternalInput").ap()
    wk_d = nc.dram_tensor("w_k", [D, D], F32, kind="ExternalInput").ap()
    wv_d = nc.dram_tensor("w_v", [D, D], F32, kind="ExternalInput").ap()
    pat_d = nc.dram_tensor("patterns", [H, M, D], F32, kind="ExternalInput").ap()
    wnq_d = nc.dram_tensor("w_nq", [E], F32, kind="ExternalInput").ap()
    wnr_d = nc.dram_tensor("w_nr", [E], F32, kind="ExternalInput").ap()
    beta_d = nc.dram_tensor("beta", [1, 1], F32, kind="ExternalInput").ap()
    wr_d = nc.dram_tensor("w_r", [E, NE], F32, kind="ExternalInput").ap()
    br_d = nc.dram_tensor("b_r", [1, NE], F32, kind="ExternalInput").ap()
    wg_d = nc.dram_tensor("w_g", [NE, E, DFF], F32, kind="ExternalInput").ap()
    wu_d = nc.dram_tensor("w_u", [NE, E, DFF], F32, kind="ExternalInput").ap()
    wd_d = nc.dram_tensor("w_d", [NE, DFF, E], F32, kind="ExternalInput").ap()
    out_d = nc.dram_tensor("out", [T, E], F32, kind="ExternalOutput").ap()
    winbf_d = nc.dram_tensor("winbf", [E, E], BF16).ap()
    g_dram = nc.dram_tensor("g_dram", [T, E], BF16).ap()

    ctx = contextlib.ExitStack()
    EC = ctx.enter_context
    with ctx:
        xnT = EC(nc.sbuf_tensor("xnT", [128, KT * T], BF16))
        mask_tok = EC(nc.sbuf_tensor("mask_tok", [128, NTP * NE], F32))
        rT_bf = EC(nc.sbuf_tensor("rT_bf", [1, T], BF16))
        ident_f = EC(nc.sbuf_tensor("ident_f", [128, 128], F32))
        ident_b = EC(nc.sbuf_tensor("ident_b", [128, 128], BF16))
        ones_col = EC(nc.sbuf_tensor("ones_col", [128, 1], BF16))
        ones_row = EC(nc.sbuf_tensor("ones_row", [1, 128], BF16))
        wnq_f = EC(nc.sbuf_tensor("wnq_f", [128, KT], F32))
        wnr_f = EC(nc.sbuf_tensor("wnr_f", [128, KT], F32))
        beta_sb = EC(nc.sbuf_tensor("beta_sb", [1, 1], F32))
        sigb_sb = EC(nc.sbuf_tensor("sigb_sb", [1, 1], F32))

        s_xp = [Sem(EC(nc.semaphore(f"s_xp{i}"))) for i in range(2)]
        s_sm = Sem(EC(nc.semaphore("s_sm")))
        s_gp = Sem(EC(nc.semaphore("s_gp")))
        s_ck = Sem(EC(nc.semaphore("s_ck")))
        s_wg = [Sem(EC(nc.semaphore(f"s_wg{i}"))) for i in range(4)]
        s_wu = [Sem(EC(nc.semaphore(f"s_wu{i}"))) for i in range(4)]
        s_wd = [Sem(EC(nc.semaphore(f"s_wd{i}"))) for i in range(2)]
        s_pat = [Sem(EC(nc.semaphore(f"s_pat{i}"))) for i in range(2)]
        s_wt = Sem(EC(nc.semaphore("s_wt")))
        s_out = Sem(EC(nc.semaphore("s_out")))
        s_gs = Sem(EC(nc.semaphore("s_gs")))
        c_tb = Sem(EC(nc.semaphore("c_tb")))
        c_ve = Sem(EC(nc.semaphore("c_ve")))
        c_sc = Sem(EC(nc.semaphore("c_sc")))
        f_gu = Sem(EC(nc.semaphore("f_gu")))
        f_wd = Sem(EC(nc.semaphore("f_wd")))
        f_pat = Sem(EC(nc.semaphore("f_pat")))
        f_win = Sem(EC(nc.semaphore("f_win")))
        f_x = Sem(EC(nc.semaphore("f_x")))

        wctx = contextlib.ExitStack()
        WC = wctx.enter_context
        NWB = 4
        wg_buf = [WC(nc.sbuf_tensor(f"wg_buf{i}", [128, KT * FC], BF16))
                  for i in range(NWB)]
        wu_buf = [WC(nc.sbuf_tensor(f"wu_buf{i}", [128, KT * FC], BF16))
                  for i in range(NWB)]
        wd_buf = [WC(nc.sbuf_tensor(f"wd_buf{i}", [128, NKF * DCH], BF16))
                  for i in range(2)]

        gu_loaded = {}
        wd_loaded = {}

        def load_gu_chunk(g, i, e, grp, c):
            par = i % NWB
            if i >= NWB:
                g.wait_ge(f_gu.h, i - NWB + 1)
            fc0 = grp * FG + c * FC
            src_g = bass.AP(wg_d.tensor, e * E * DFF + fc0,
                            [[DFF, 128], [DFF * 128, KT], [1, FC]])
            dst_g = bass.AP(wg_buf[par], 0, [[KT * FC, 128], [FC, KT], [1, FC]])
            g.dma_start(dst_g, src_g).then_inc(s_wg[par].h, 16)
            s_wg[par].n += 16
            src_u = bass.AP(wu_d.tensor, e * E * DFF + fc0,
                            [[DFF, 128], [DFF * 128, KT], [1, FC]])
            dst_u = bass.AP(wu_buf[par], 0, [[KT * FC, 128], [FC, KT], [1, FC]])
            g.dma_start(dst_u, src_u).then_inc(s_wu[par].h, 16)
            s_wu[par].n += 16
            gu_loaded[i] = (s_wg[par].n, s_wu[par].n)

        def load_wd_piece(g, j, e, grp, dch):
            par = j % 2
            if j >= 2:
                g.wait_ge(f_wd.h, j - 1)
            srcp = bass.AP(wd_d.tensor,
                           e * DFF * E + (grp * FG) * E + dch * DCH,
                           [[E, 128], [E * 128, NKF], [1, DCH]])
            dstp = bass.AP(wd_buf[par], 0,
                           [[NKF * DCH, 128], [DCH, NKF], [1, DCH]])
            g.dma_start(dstp, srcp).then_inc(s_wd[par].h, 16)
            s_wd[par].n += 16
            wd_loaded[j] = s_wd[par].n

        with contextlib.ExitStack() as pctx:
            PC = pctx.enter_context
            x_tile = [PC(nc.sbuf_tensor(f"x_tile{i}", [128, E], F32)) for i in range(2)]
            xn_f = [PC(nc.sbuf_tensor(f"xn_f{i}", [128, E], F32)) for i in range(2)]
            sq_f = [PC(nc.sbuf_tensor(f"sq_f{i}", [128, E], F32)) for i in range(2)]
            strip = [PC(nc.sbuf_tensor(f"strip{i}", [128, E], F32)) for i in range(2)]
            wr_sb = PC(nc.sbuf_tensor("wr_sb", [128, KT * NE], F32))
            br_row = PC(nc.sbuf_tensor("br_row", [1, NE], F32))
            br_hi = PC(nc.sbuf_tensor("br_hi", [1, NE], BF16))
            br_hf = PC(nc.sbuf_tensor("br_hf", [1, NE], F32))
            br_lo = PC(nc.sbuf_tensor("br_lo", [1, NE], BF16))
            brB = PC(nc.sbuf_tensor("brB", [128, NE], F32))
            logit_tok = PC(nc.sbuf_tensor("logit_tok", [128, NTP * NE], F32))
            rT_f = PC(nc.sbuf_tensor("rT_f", [1, T], F32))
            r_tok = PC(nc.sbuf_tensor("r_tok", [128, NTP], F32))
            st_sum = PC(nc.sbuf_tensor("st_sum", [128, 1], F32))
            st_ss = PC(nc.sbuf_tensor("st_ss", [128, 1], F32))
            st_m = PC(nc.sbuf_tensor("st_m", [128, 1], F32))
            st_var = PC(nc.sbuf_tensor("st_var", [128, 1], F32))
            st_t = PC(nc.sbuf_tensor("st_t", [128, 1], F32))
            st_rstd = PC(nc.sbuf_tensor("st_rstd", [128, 1], F32))
            mx = PC(nc.sbuf_tensor("mx", [128, 1], F32))
            nmx = PC(nc.sbuf_tensor("nmx", [128, 1], F32))
            e4 = PC(nc.sbuf_tensor("e4", [128, NE], F32))
            s4 = PC(nc.sbuf_tensor("s4", [128, 1], F32))
            tw = PC(nc.sbuf_tensor("tw", [128, 1], F32))
            ohs = PC(nc.sbuf_tensor("ohs", [128, NE], F32))
            run = PC(nc.sbuf_tensor("run", [128, 1], F32))
            nrun = PC(nc.sbuf_tensor("nrun", [128, 1], F32))

            tp_ps = [PC(nc.psum_tensor(f"tp_ps{i}", [128, 512], F32)) for i in range(2)]
            lt_ps = PC(nc.psum_tensor("lt_ps", [128, NTP * NE], F32))
            rt_ps = PC(nc.psum_tensor("rt_ps", [1, T], F32))
            br_ps = PC(nc.psum_tensor("br_ps", [128, NE], F32))

            blk = PC(nc.Block("P"))

            x_loaded = [0] * NTP
            rattn_done = [0] * NTP
            winbf_cnt = [0]

            def p_sync0(sy):
                sy.dma_start(
                    bass.AP(wr_sb, 0, [[KT * NE, 128], [NE, KT], [1, NE]]),
                    bass.AP(wr_d.tensor, 0, [[NE, 128], [NE * 128, KT], [1, NE]]),
                    allow_slow_non_contiguous=True,
                ).then_inc(s_sm.h, 16); s_sm.n += 16
                sy.dma_start(br_row[:], br_d).then_inc(s_sm.h, 16); s_sm.n += 16
                sy.dma_start(
                    wnr_f[:], wnr_d.rearrange("(kt p) -> p kt", p=128),
                    allow_slow_non_contiguous=True,
                ).then_inc(s_sm.h, 16); s_sm.n += 16
                sy.dma_start(
                    wnq_f[:], wnq_d.rearrange("(kt p) -> p kt", p=128),
                    allow_slow_non_contiguous=True,
                ).then_inc(s_sm.h, 16); s_sm.n += 16
                sy.dma_start(beta_sb[:], beta_d).then_inc(s_sm.h, 16); s_sm.n += 16
                for tp in range(2):
                    sy.dma_start(
                        x_tile[tp][:],
                        bass.AP(x_d.tensor, tp * 128 * E, [[E, 128], [1, E]]),
                    ).then_inc(s_xp[tp % 2].h, 16); s_xp[tp % 2].n += 16
                    x_loaded[tp] = s_xp[tp % 2].n

            blk.sync(p_sync0)
            sm_all = s_sm.n

            def p_gp0(g):
                g.memset(ident_f[:], 0.0)
                g.drain()
                g.affine_select(out=ident_f[:], in_=ident_f[:],
                                compare_op=OP.not_equal, fill=1.0, base=0,
                                pattern=[[-1, 128]], channel_multiplier=1)
                g.memset(ident_b[:], 0.0)
                g.drain()
                g.affine_select(out=ident_b[:], in_=ident_b[:],
                                compare_op=OP.not_equal, fill=1.0, base=0,
                                pattern=[[-1, 128]], channel_multiplier=1)
                g.memset(ones_col[:], 1.0)
                s_ck.inc(g.memset(ones_row[:], 1.0))
                for i in range(KT):
                    g.dma_start(
                        bass.AP(winbf_d.tensor, i * 128 * E, [[E, 128], [1, E]]),
                        bass.AP(win_d.tensor, i * 128 * E, [[E, 128], [1, E]]),
                    ).then_inc(s_gp.h, 16); s_gp.n += 16

            blk.gpsimd(p_gp0)
            consts_done = 1

            ev_done = {}
            for tp in range(NTP):
                par = tp % 2

                def p_sc_sq(s, tp=tp, par=par):
                    if tp == 0:
                        s.wait_ge(s_sm.h, sm_all)
                        s.activation(sigb_sb[:], beta_sb[:], AF.Sigmoid)
                    s.wait_ge(s_xp[tp % 2].h, x_loaded[tp])
                    if tp >= 2:
                        s.wait_ge(c_ve.h, ev_done[("ssred", tp - 2)])
                    c_sc.inc(s.activation(sq_f[par][:], x_tile[par][:], AF.Square))

                blk.scalar(p_sc_sq)
                sq_ready = c_sc.n

                def p_ve_stats(v, tp=tp, par=par):
                    v.wait_ge(s_xp[tp % 2].h, x_loaded[tp])
                    if tp >= 1:
                        v.wait_ge(c_sc.h, rattn_done[tp - 1])
                    v.tensor_reduce(st_sum[:], x_tile[par][:], AX.X, OP.add)
                    v.wait_ge(c_sc.h, sq_ready)
                    v.tensor_reduce(st_ss[:], sq_f[par][:], AX.X, OP.add)
                    c_ve.inc(v.drain())
                    _vg(v)
                    v.tensor_scalar(st_m[:], st_sum[:], 1.0 / E, None, OP.mult)
                    _vg(v)
                    v.tensor_tensor(st_t[:], st_m[:], st_m[:], OP.mult)
                    v.tensor_scalar(st_var[:], st_ss[:], 1.0 / E, None, OP.mult)
                    _vg(v)
                    v.tensor_tensor(st_var[:], st_var[:], st_t[:], OP.subtract)
                    _vg(v)
                    v.tensor_scalar(st_t[:], st_var[:], 1e-5, None, OP.add)
                    _vg(v)
                    v.reciprocal(st_t[:], st_t[:])
                    c_ve.inc(v.drain())

                blk.vector(p_ve_stats)
                ev_done[("ssred", tp)] = c_ve.n - 1
                rstd2_ready = c_ve.n

                def p_sc_rstd(s, tp=tp):
                    s.wait_ge(c_ve.h, rstd2_ready)
                    c_sc.inc(s.sqrt(st_rstd[:], st_t[:]))

                blk.scalar(p_sc_rstd)
                rstd_ready = c_sc.n

                def p_ve_xn(v, tp=tp, par=par):
                    v.wait_ge(c_sc.h, rstd_ready)
                    if tp >= 2:
                        v.wait_ge(c_tb.h, ev_done[("trq", tp - 2, 3)])
                    f_x.inc(v.tensor_scalar(xn_f[par][:], x_tile[par][:], st_m[:],
                                            st_rstd[:], OP.subtract, OP.mult))
                    v.tensor_tensor(st_t[:], st_t[:], st_var[:], OP.mult)
                    _vg(v)
                    v.tensor_scalar(st_t[:], st_t[:], 1e-6, None, OP.add)
                    _vg(v)
                    v.reciprocal(st_t[:], st_t[:])
                    c_ve.inc(v.drain())

                blk.vector(p_ve_xn)
                xn_ready = c_ve.n
                x_free = f_x.n

                def p_sc_rattn(s, tp=tp):
                    s.wait_ge(c_ve.h, xn_ready)
                    c_sc.inc(s.sqrt(r_tok[:, tp:tp + 1], st_t[:]))

                blk.scalar(p_sc_rattn)
                rattn_ready = c_sc.n
                rattn_done[tp] = rattn_ready
                ev_done[("rattn", tp)] = rattn_ready

                def p_sy_next(sy, tp=tp):
                    sy.wait_ge(f_x.h, x_free)
                    if tp + 2 < NTP:
                        sy.dma_start(
                            x_tile[tp % 2][:],
                            bass.AP(x_d.tensor, (tp + 2) * 128 * E,
                                    [[E, 128], [1, E]]),
                        ).then_inc(s_xp[tp % 2].h, 16); s_xp[tp % 2].n += 16
                        x_loaded[tp + 2] = s_xp[tp % 2].n

                if tp + 2 < NTP:
                    blk.sync(p_sy_next)

                for q in range(4):
                    def p_tb_trq(t, tp=tp, par=par, q=q):
                        if tp == 0 and q == 0:
                            t.wait_ge(s_gp.h, consts_done)
                        t.wait_ge(c_ve.h, xn_ready)
                        qi = tp * 4 + q
                        if qi >= 2:
                            t.wait_ge(c_ve.h, ev_done[("evq", qi - 2)])
                        ps = tp_ps[qi % 2]
                        for j in range(4):
                            kt = q * 4 + j
                            t.transpose(ps[:, j * 128:(j + 1) * 128],
                                        xn_f[par][:, kt * 128:(kt + 1) * 128],
                                        ident_f[:])
                        c_tb.inc(t.drain())

                    blk.tensor(p_tb_trq)
                    ev_done[("trq", tp, q)] = c_tb.n

                    def p_ve_evq(v, tp=tp, q=q):
                        qi = tp * 4 + q
                        v.wait_ge(c_tb.h, ev_done[("trq", tp, q)])
                        if tp >= 2 and q == 0:
                            v.wait_ge(c_tb.h, ev_done[("rmm", tp - 2)])
                        ps = tp_ps[qi % 2]
                        for j in range(4):
                            kt = q * 4 + j
                            v.tensor_copy(
                                bass.AP(xnT, kt * T + tp * 128,
                                        [[KT * T, 128], [1, 128]]),
                                ps[:, j * 128:(j + 1) * 128])
                        c_ve.inc(v.tensor_copy(
                            strip[tp % 2][:, q * 512:(q + 1) * 512], ps[:]))

                    blk.vector(p_ve_evq)
                    ev_done[("evq", tp * 4 + q)] = c_ve.n

                def p_tb_router(t, tp=tp):
                    t.wait_ge(c_ve.h, ev_done[("evq", tp * 4 + 3)])
                    t.wait_ge(s_sm.h, sm_all)
                    for kt in range(KT):
                        t.matmul(lt_ps[:, tp * NE:(tp + 1) * NE],
                                 strip[tp % 2][:, kt * 128:(kt + 1) * 128],
                                 wr_sb[:, kt * NE:(kt + 1) * NE],
                                 start=(kt == 0), stop=(kt == KT - 1))
                    t.wait_ge(c_sc.h, ev_done[("rattn", tp)])
                    t.transpose(rt_ps[:, tp * 128:(tp + 1) * 128],
                                r_tok[:, tp:tp + 1], ident_f[:])
                    c_tb.inc(t.drain())

                blk.tensor(p_tb_router)
                ev_done[("rmm", tp)] = c_tb.n

            def p_ve_br(v):
                v.wait_ge(s_sm.h, sm_all)
                v.tensor_copy(br_hi[:], br_row[:])
                _vg(v)
                v.tensor_copy(br_hf[:], br_hi[:])
                _vg(v)
                v.tensor_tensor(br_lo[:], br_row[:], br_hf[:], OP.subtract)
                c_ve.inc(v.drain())

            blk.vector(p_ve_br)
            br_split = c_ve.n

            def p_tb_br(t):
                t.wait_ge(c_ve.h, br_split)
                t.matmul(br_ps[:], ones_row[:], br_hi[:], start=True, stop=False)
                t.matmul(br_ps[:], ones_row[:], br_lo[:], start=False, stop=True)
                c_tb.inc(t.drain())

            blk.tensor(p_tb_br)
            brB_done = c_tb.n

            def p_ve_fin(v):
                v.wait_ge(c_tb.h, brB_done)
                v.tensor_copy(brB[:], br_ps[:])
                v.tensor_copy(rT_f[:], rt_ps[:])
                _vg(v)
                sb_b = bass.AP(sigb_sb, 0, [[1, 1], [0, T]])
                v.tensor_tensor(rT_f[:], rT_f[:], sb_b, OP.mult)
                _vg(v)
                v.tensor_copy(rT_bf[:], rT_f[:])
                _vg(v)
                for tp in range(NTP):
                    brB_b = bass.AP(brB, 0, [[NE, 128], [1, NE]])
                    v.tensor_tensor(logit_tok[:, tp * NE:(tp + 1) * NE],
                                    lt_ps[:, tp * NE:(tp + 1) * NE],
                                    brB_b, OP.add)
                c_ve.inc(v.drain())

            blk.vector(p_ve_fin)
            logits_done = c_ve.n

            for tp in range(NTP):
                def p_ve_m1(v, tp=tp):
                    if tp == 0:
                        v.wait_ge(c_ve.h, logits_done)
                    L = logit_tok[:, tp * NE:(tp + 1) * NE]
                    v.tensor_reduce(mx[:], L, AX.X, OP.max)
                    _vg(v)
                    v.tensor_scalar(nmx[:], mx[:], -1.0, None, OP.mult)
                    c_ve.inc(v.drain())

                blk.vector(p_ve_m1)
                m1 = c_ve.n

                def p_sc_exp(s, tp=tp):
                    s.wait_ge(c_ve.h, m1)
                    c_sc.inc(s.activation(e4[:], logit_tok[:, tp * NE:(tp + 1) * NE],
                                          AF.Exp, bias=nmx[:], scale=1.0))

                blk.scalar(p_sc_exp)
                expd = c_sc.n

                def p_ve_m2(v, tp=tp):
                    L = logit_tok[:, tp * NE:(tp + 1) * NE]
                    v.wait_ge(c_sc.h, expd)
                    v.tensor_reduce(s4[:], e4[:], AX.X, OP.add)
                    _vg(v)
                    v.reciprocal(tw[:], s4[:])
                    for e in range(NE):
                        v.tensor_tensor(ohs[:, e:e + 1], L[:, e:e + 1], mx[:],
                                        OP.is_ge)
                    _vg(v)
                    v.tensor_copy(run[:], ohs[:, 0:1])
                    _vg(v)
                    for e in range(1, NE):
                        v.tensor_scalar(nrun[:], run[:], -1.0, 1.0, OP.mult, OP.add)
                        _vg(v)
                        v.tensor_tensor(ohs[:, e:e + 1], ohs[:, e:e + 1], nrun[:],
                                        OP.mult)
                        _vg(v)
                        if e < NE - 1:
                            v.tensor_tensor(run[:], run[:], ohs[:, e:e + 1], OP.max)
                            _vg(v)
                    for e in range(NE):
                        v.tensor_tensor(
                            mask_tok[:, tp * NE + e:tp * NE + e + 1],
                            ohs[:, e:e + 1], tw[:], OP.mult)
                    c_ve.inc(v.drain())

                blk.vector(p_ve_m2)

        with contextlib.ExitStack() as fctx:
            FC_ = fctx.enter_context
            h_buf = FC_(nc.sbuf_tensor("h_buf", [128, NKF * T], BF16))
            y_sb = FC_(nc.sbuf_tensor("y_sb", [128, NTP * E], F32))
            sg_buf = [FC_(nc.sbuf_tensor(f"sg_buf{i}", [128, 512], F32))
                      for i in range(3)]
            g_stage = FC_(nc.sbuf_tensor("g_stage", [128, E], BF16))

            pg = [FC_(nc.psum_tensor(f"pg{i}", [128, 512], F32)) for i in range(3)]
            pu = [FC_(nc.psum_tensor(f"pu{i}", [128, 512], F32)) for i in range(3)]

            blk = FC_(nc.Block("F"))

            seq_inc = {}
            seq_free = {}
            gchunk = 0
            gpiece = 0
            seq = 0

            groups = [(e, grp) for e in range(NE) for grp in range(NGRP)]

            for egi, (e, grp) in enumerate(groups):
                if True:
                    def f_gp_load(g, egi=egi, e=e, grp=grp, gchunk0=gchunk,
                                  gpiece0=gpiece):
                        if egi == 0:
                            for c in range(FG // FC):
                                load_gu_chunk(g, gchunk0 + c, e, grp, c)
                        for dch in range(E // DCH):
                            load_wd_piece(g, gpiece0 + dch, e, grp, dch)

                    blk.gpsimd(f_gp_load)
                    grp_groups = []
                    for c in range(FG // FC):
                        for fp in range(FC // 128):
                            for th in range(2):
                                grp_groups.append((seq, gchunk + c, c, fp, th))
                                seq += 1
                    pend_evac = []

                    def emit_gu_group(gi_tuple):
                        sq, ichunk, c, fp, th = gi_tuple
                        bank = sq % 3
                        cpar = ichunk % NWB

                        def f_tb(t, sq=sq, ichunk=ichunk, c=c, fp=fp, th=th,
                                 bank=bank, cpar=cpar):
                            if fp == 0 and th == 0:
                                t.wait_ge(s_wg[cpar].h, gu_loaded[ichunk][0])
                                t.wait_ge(s_wu[cpar].h, gu_loaded[ichunk][1])
                            if sq - 3 in seq_free:
                                t.wait_ge(c_ve.h, seq_free[sq - 3])
                            for kt in range(KT):
                                t.matmul(
                                    pg[bank][:],
                                    wg_buf[cpar][:, kt * FC + fp * 128:
                                                 kt * FC + fp * 128 + 128],
                                    xnT[:, kt * T + th * TH:
                                        kt * T + th * TH + TH],
                                    start=(kt == 0), stop=(kt == KT - 1))
                            for kt in range(KT):
                                ins = t.matmul(
                                    pu[bank][:],
                                    wu_buf[cpar][:, kt * FC + fp * 128:
                                                 kt * FC + fp * 128 + 128],
                                    xnT[:, kt * T + th * TH:
                                        kt * T + th * TH + TH],
                                    start=(kt == 0), stop=(kt == KT - 1))
                                if kt == KT - 2 and fp == 0 and th == 1 and c >= 1:
                                    f_gu.inc(ins)
                                if kt == KT - 1:
                                    c_tb.inc(ins)

                        blk.tensor(f_tb)
                        seq_inc[sq] = c_tb.n

                    def emit_gu_evac(gi_tuple, gate):
                        sq, ichunk, c, fp, th, = gi_tuple
                        bank = sq % 3

                        def f_sc(s, sq=sq, bank=bank, gate=gate):
                            s.wait_ge(c_tb.h, gate)
                            c_sc.inc(s.activation(sg_buf[bank][:], pg[bank][:],
                                                  AF.Silu))

                        blk.scalar(f_sc)
                        silu_done = c_sc.n

                        def f_ve(v, sq=sq, c=c, fp=fp, th=th, bank=bank,
                                 silu_done=silu_done):
                            v.wait_ge(c_sc.h, silu_done)
                            kf = c * (FC // 128) + fp
                            c_ve.inc(v.tensor_tensor(
                                h_buf[:, kf * T + th * TH:
                                      kf * T + th * TH + TH],
                                sg_buf[bank][:], pu[bank][:], OP.mult))

                        blk.vector(f_ve)
                        seq_free[sq] = c_ve.n

                    for gi_tuple in grp_groups:
                        emit_gu_group(gi_tuple)
                        pend_evac.append(gi_tuple)
                        if len(pend_evac) >= 2:
                            g0 = pend_evac.pop(0)
                            emit_gu_evac(g0, seq_inc[g0[0] + 1])

                    def f_tb_gu_end(t):
                        c_tb.inc(t.drain())

                    blk.tensor(f_tb_gu_end)
                    g0 = pend_evac.pop(0)
                    emit_gu_evac(g0, c_tb.n)
                    h_ready_val = c_ve.n

                    y_groups = []
                    for dch in range(E // DCH):
                        for tp2 in range(NTP):
                            y_groups.append((seq, gpiece + dch, dch, tp2))
                            seq += 1
                    pend_y = []
                    first_y_seq = y_groups[0][0]

                    def emit_y_group(gt):
                        sq, jpiece, dch, tp2 = gt
                        bank = sq % 3
                        dpar = jpiece % 2

                        def f_tb(t, sq=sq, jpiece=jpiece, dch=dch, tp2=tp2,
                                 bank=bank, dpar=dpar):
                            if tp2 == 0:
                                t.wait_ge(s_wd[dpar].h, wd_loaded[jpiece])
                            if sq == first_y_seq:
                                t.wait_ge(c_ve.h, h_ready_val)
                            if sq - 3 in seq_free:
                                t.wait_ge(c_ve.h, seq_free[sq - 3])
                            for kf in range(NKF):
                                ins = t.matmul(
                                    pg[bank][:],
                                    h_buf[:, kf * T + tp2 * 128:
                                          kf * T + tp2 * 128 + 128],
                                    wd_buf[dpar][:, kf * DCH:(kf + 1) * DCH],
                                    start=(kf == 0), stop=(kf == NKF - 1))
                                if kf == NKF - 2 and tp2 == 1 and dch >= 1:
                                    f_wd.inc(ins)
                                if kf == NKF - 2 and tp2 == 1 and dch == 0:
                                    f_gu.inc(ins)
                                if kf == NKF - 1:
                                    c_tb.inc(ins)

                        blk.tensor(f_tb)
                        seq_inc[sq] = c_tb.n

                    def emit_y_evac(gt, gate, e=e, grp=grp):
                        sq, jpiece, dch, tp2 = gt
                        bank = sq % 3

                        def f_ve(v, sq=sq, dch=dch, tp2=tp2, bank=bank,
                                 gate=gate):
                            v.wait_ge(c_tb.h, gate)
                            ysl = y_sb[:, tp2 * E + dch * DCH:
                                       tp2 * E + (dch + 1) * DCH]
                            msl = mask_tok[:, tp2 * NE + e:tp2 * NE + e + 1]
                            if e == 0 and grp == 0:
                                c_ve.inc(v.tensor_scalar(ysl, pg[bank][:],
                                                         msl, None, OP.mult))
                            else:
                                c_ve.inc(v.scalar_tensor_tensor(
                                    ysl, pg[bank][:], msl, ysl,
                                    OP.mult, OP.add))

                        blk.vector(f_ve)
                        seq_free[sq] = c_ve.n

                    def f_gp_next(g, egi=egi, gchunk0=gchunk):
                        if egi + 1 < len(groups):
                            e2, grp2 = groups[egi + 1]
                            nxt0 = gchunk0 + FG // FC
                            for c in range(FG // FC):
                                load_gu_chunk(g, nxt0 + c, e2, grp2, c)

                    blk.gpsimd(f_gp_next)

                    for gt in y_groups:
                        emit_y_group(gt)
                        pend_y.append(gt)
                        if len(pend_y) >= 2:
                            g0 = pend_y.pop(0)
                            emit_y_evac(g0, seq_inc[g0[0] + 1])

                    def f_tb_end(t):
                        c_tb.inc(t.drain())
                        t.nop().then_inc(f_wd.h, 1); f_wd.n += 1

                    blk.tensor(f_tb_end)
                    end_gate = c_tb.n
                    g0 = pend_y.pop(0)
                    emit_y_evac(g0, end_gate)

                    gchunk += FG // FC
                    gpiece += E // DCH

            y_done_v = c_ve.n

            def f_sc_g(s):
                s.wait_ge(c_ve.h, y_done_v)
                for tp2 in range(NTP):
                    if tp2 >= 1:
                        s.wait_ge(s_gs.h, 16 * tp2)
                    for cc in range(4):
                        ins = s.activation(
                            g_stage[:, cc * 512:(cc + 1) * 512],
                            y_sb[:, tp2 * E + cc * 512:tp2 * E + (cc + 1) * 512],
                            AF.Silu)
                        if cc == 3:
                            c_sc.inc(ins)
                    s.wait_ge(c_sc.h, c_sc.n)
                    s.dma_start(
                        bass.AP(g_dram.tensor, tp2 * 128 * E, [[E, 128], [1, E]]),
                        g_stage[:],
                    ).then_inc(s_gs.h, 16); s_gs.n += 16
                s.wait_ge(s_gs.h, s_gs.n)

            blk.scalar(f_sc_g)

        wctx.pop_all().close()

        with contextlib.ExitStack() as actx:
            AC = actx.enter_context
            afT = AC(nc.sbuf_tensor("afT", [128, KT * T], BF16))
            sqbuf = AC(nc.sbuf_tensor("sqbuf", [128, T], BF16))
            a12ctx = contextlib.ExitStack()
            AC = a12ctx.enter_context
            projT = AC(nc.sbuf_tensor("projT", [128, KT * T], BF16))
            rB_sb = [AC(nc.sbuf_tensor(f"rB_sb{i}", [128, TH], F32))
                     for i in range(2)]
            wq_bf = AC(nc.sbuf_tensor("wq_bf", [128, 128], BF16))
            wk_bf = AC(nc.sbuf_tensor("wk_bf", [128, 128], BF16))
            wv_bf = AC(nc.sbuf_tensor("wv_bf", [128, 128], BF16))
            wqT = AC(nc.sbuf_tensor("wqT", [128, 128], BF16))
            wkT = AC(nc.sbuf_tensor("wkT", [128, 128], BF16))
            wvT = AC(nc.sbuf_tensor("wvT", [128, 128], BF16))
            pat_bf = [AC(nc.sbuf_tensor(f"pat_bf{i}", [128, NMT * 128], BF16))
                      for i in range(2)]
            patT_sb = [AC(nc.sbuf_tensor(f"patT_sb{i}", [128, NMT * 128], BF16))
                       for i in range(2)]
            kT_sb = [AC(nc.sbuf_tensor(f"kT_sb{i}", [128, M], BF16))
                     for i in range(2)]
            v_sb = [AC(nc.sbuf_tensor(f"v_sb{i}", [128, NMT * 128], BF16))
                    for i in range(2)]
            q_bf = [AC(nc.sbuf_tensor(f"q_bf{i}", [128, M], BF16))
                    for i in range(2)]
            eT_sb = AC(nc.sbuf_tensor("eT_sb", [128, NMT * TH], BF16))
            scbuf = AC(nc.sbuf_tensor("scbuf", [128, NMT * TH], BF16))
            den_bf = AC(nc.sbuf_tensor("den_bf", [1, TH], BF16))
            rden_f = AC(nc.sbuf_tensor("rden_f", [128, TH], F32))
            win_buf = [AC(nc.sbuf_tensor(f"win_buf{i}", [128, KT * 512], BF16))
                       for i in range(2)]

            with contextlib.ExitStack() as a0ctx:
                A0 = a0ctx.enter_context
                pj_ps = [A0(nc.psum_tensor(f"pj_ps{i}", [128, 512], F32))
                         for i in range(2)]
                small_ps = A0(nc.psum_tensor("small_ps", [128, 512], F32))
                smallb_ps = A0(nc.psum_tensor("smallb_ps", [128, 384], BF16))

                blk = A0(nc.Block("A0"))

                def a_gp0(g):
                    g.dma_start(wq_bf[:], wq_d).then_inc(s_gp.h, 16); s_gp.n += 16
                    g.dma_start(wk_bf[:], wk_d).then_inc(s_gp.h, 16); s_gp.n += 16
                    g.dma_start(wv_bf[:], wv_d).then_inc(s_gp.h, 16); s_gp.n += 16
                    for h in range(2):
                        src = bass.AP(pat_d.tensor, h * M * D,
                                      [[D, 128], [D * 128, NMT], [1, D]])
                        dst = bass.AP(pat_bf[h], 0,
                                      [[NMT * 128, 128], [128, NMT], [1, D]])
                        g.dma_start(dst, src).then_inc(s_pat[h].h, 16)
                        s_pat[h].n += 16

                blk.gpsimd(a_gp0)
                smalls = s_gp.n

                def a_tb0(t):
                    t.wait_ge(s_gp.h, smalls)
                    t.transpose(smallb_ps[:, 0:128], wq_bf[:], ident_b[:])
                    t.transpose(smallb_ps[:, 128:256], wk_bf[:], ident_b[:])
                    t.transpose(smallb_ps[:, 256:384], wv_bf[:], ident_b[:])
                    t.matmul(pj_ps[0][:], ones_row[:], rT_bf[0:1, 0:TH],
                             start=True, stop=True)
                    t.matmul(pj_ps[1][:], ones_row[:], rT_bf[0:1, TH:T],
                             start=True, stop=True)
                    c_tb.inc(t.drain())

                blk.tensor(a_tb0)
                a0t = c_tb.n

                def a_ve0(v):
                    v.wait_ge(c_tb.h, a0t)
                    v.tensor_copy(wqT[:], smallb_ps[:, 0:128])
                    v.tensor_copy(wkT[:], smallb_ps[:, 128:256])
                    v.tensor_copy(wvT[:], smallb_ps[:, 256:384])
                    v.tensor_copy(rB_sb[0][:], pj_ps[0][:])
                    v.tensor_copy(rB_sb[1][:], pj_ps[1][:])
                    c_ve.inc(v.drain())

                blk.vector(a_ve0)
                a0v = c_ve.n

                win_loaded = [0] * 4
                pj_free = {}
                wqfold = [0] * 4
                pji = 0
                for och in range(4):
                    def a_sy_win2(sy, och=och):
                        if och == 0:
                            sy.wait_ge(s_gp.h, winbf_cnt[0])
                        if och >= 2:
                            sy.wait_ge(f_win.h, och - 1)
                        for kt in range(KT):
                            src = bass.AP(
                                winbf_d.tensor, (och * 512) * E + kt * 128,
                                [[E, 512], [1, 128]])
                            sy.dma_start_transpose(
                                win_buf[och % 2][:, kt * 512:(kt + 1) * 512],
                                src).then_inc(s_wt.h, 16)
                            s_wt.n += 16

                    blk.sync(a_sy_win2)
                    win_loaded[och] = s_wt.n

                    def a_ve_fold(v, och=och):
                        v.wait_ge(s_wt.h, win_loaded[och])
                        for kt in range(KT):
                            v.tensor_scalar(
                                win_buf[och % 2][:, kt * 512:(kt + 1) * 512],
                                win_buf[och % 2][:, kt * 512:(kt + 1) * 512],
                                wnq_f[:, kt:kt + 1], None, OP.mult)
                        c_ve.inc(v.drain())

                    blk.vector(a_ve_fold)
                    wqfold[och] = c_ve.n

                    for op in range(4):
                        for th in range(2):
                            cur = pji
                            pji += 1

                            def a_tb_pj(t, och=och, op=op, th=th, cur=cur):
                                if op == 0 and th == 0:
                                    t.wait_ge(c_ve.h, wqfold[och])
                                if cur < 2:
                                    t.wait_ge(c_ve.h, a0v)
                                else:
                                    t.wait_ge(c_ve.h, pj_free[cur - 2])
                                for kt in range(KT):
                                    t.matmul(
                                        pj_ps[cur % 2][:],
                                        win_buf[och % 2][:, kt * 512 + op * 128:
                                                         kt * 512 + op * 128 + 128],
                                        xnT[:, kt * T + th * TH:
                                            kt * T + th * TH + TH],
                                        start=(kt == 0), stop=(kt == KT - 1))
                                c_tb.inc(t.drain())
                                if op == 3 and th == 1:
                                    f_win.inc(t.nop())

                            blk.tensor(a_tb_pj)
                            pjd = c_tb.n

                            def a_ve_pj(v, och=och, op=op, th=th, cur=cur,
                                        pjd=pjd):
                                v.wait_ge(c_tb.h, pjd)
                                ot = och * 4 + op
                                c_ve.inc(v.tensor_tensor(
                                    projT[:, ot * T + th * TH:
                                          ot * T + th * TH + TH],
                                    pj_ps[cur % 2][:], rB_sb[th][:], OP.mult))

                            blk.vector(a_ve_pj)
                            pj_free[cur] = c_ve.n

            with contextlib.ExitStack() as a2ctx:
                A2 = a2ctx.enter_context
                patT_ps = A2(nc.psum_tensor("patT_ps", [128, NMT * 128], BF16))
                kv_ps = A2(nc.psum_tensor("kv_ps", [128, M], F32))
                sc_ps = [A2(nc.psum_tensor(f"sc_ps{i}", [128, TH], F32))
                         for i in range(4)]
                ao_ps = A2(nc.psum_tensor("ao_ps", [128, TH], F32))

                blk = A2(nc.Block("A2"))

                G = {}
                G_scfree = {}

                def prep_pat(h):
                    hp = h % 2

                    def tb(t, h=h, hp=hp):
                        t.wait_ge(s_pat[hp].h, 16 * (h // 2 + 1))
                        if h >= 1:
                            t.wait_ge(c_ve.h, G[("patv", h - 1)])
                        for mt in range(NMT):
                            t.transpose(patT_ps[:, mt * 128:(mt + 1) * 128],
                                        pat_bf[hp][:, mt * 128:(mt + 1) * 128],
                                        ident_b[:])
                        c_tb.inc(t.drain())
                        f_pat.inc(t.nop())

                    blk.tensor(tb)
                    gate = c_tb.n

                    def gp(g, h=h):
                        if h + 2 < H:
                            g.wait_ge(f_pat.h, h + 1)
                            srcp = bass.AP(pat_d.tensor, (h + 2) * M * D,
                                           [[D, 128], [D * 128, NMT], [1, D]])
                            dstp = bass.AP(pat_bf[h % 2], 0,
                                           [[NMT * 128, 128], [128, NMT], [1, D]])
                            g.dma_start(dstp, srcp).then_inc(s_pat[h % 2].h, 16)
                            s_pat[h % 2].n += 16

                    blk.gpsimd(gp)

                    def ve(v, h=h, hp=hp, gate=gate):
                        v.wait_ge(c_tb.h, gate)
                        v.tensor_copy(patT_sb[hp][:], patT_ps[:])
                        c_ve.inc(v.drain())

                    blk.vector(ve)
                    G[("patv", h)] = c_ve.n

                def prep_kT(h):
                    hp = h % 2

                    def tb(t, h=h, hp=hp):
                        t.wait_ge(c_ve.h, G[("patv", h)])
                        if h >= 1:
                            t.wait_ge(c_ve.h, G[("qev", h - 1, 1)])
                        t.matmul(kv_ps[:, 0:512], wkT[:],
                                 patT_sb[hp][:, 0:512], start=True, stop=True)
                        t.matmul(kv_ps[:, 512:M], wkT[:],
                                 patT_sb[hp][:, 512:M], start=True, stop=True)
                        c_tb.inc(t.drain())

                    blk.tensor(tb)
                    gate = c_tb.n

                    def ve(v, hp=hp, gate=gate):
                        v.wait_ge(c_tb.h, gate)
                        v.tensor_copy(kT_sb[hp][:], kv_ps[:])
                        c_ve.inc(v.drain())

                    blk.vector(ve)
                    G[("ktv", h)] = c_ve.n

                def prep_v(h):
                    hp = h % 2

                    def tb(t, h=h, hp=hp):
                        t.wait_ge(c_ve.h, G[("ktv", h)])
                        for mt in range(NMT):
                            t.matmul(kv_ps[:, mt * 128:(mt + 1) * 128],
                                     patT_sb[hp][:, mt * 128:(mt + 1) * 128],
                                     wvT[:], start=True, stop=True)
                        c_tb.inc(t.drain())

                    blk.tensor(tb)
                    gate = c_tb.n

                    def ve(v, hp=hp, gate=gate):
                        v.wait_ge(c_tb.h, gate)
                        v.tensor_copy(v_sb[hp][:], kv_ps[:])
                        c_ve.inc(v.drain())

                    blk.vector(ve)
                    G[("vv", h)] = c_ve.n

                def prep_q(h):
                    hp = h % 2

                    def tb(t, h=h, hp=hp):
                        t.wait_ge(c_ve.h, G[("vv", h)])
                        for th in range(2):
                            t.matmul(kv_ps[:, th * TH:(th + 1) * TH], wqT[:],
                                     projT[:, h * T + th * TH:
                                           h * T + th * TH + TH],
                                     start=True, stop=True)
                        c_tb.inc(t.drain())

                    blk.tensor(tb)
                    gate = c_tb.n

                    def ve(v, hp=hp, gate=gate):
                        v.wait_ge(c_tb.h, gate)
                        v.tensor_copy(q_bf[hp][:], kv_ps[:])
                        c_ve.inc(v.drain())

                    blk.vector(ve)
                    G[("qev", h, 0)] = c_ve.n
                    G[("qev", h, 1)] = c_ve.n

                def score_scexp(h, th):
                    hp = h % 2
                    for mp in range(4):
                        def tb(t, h=h, th=th, hp=hp, mp=mp):
                            if mp == 0:
                                t.wait_ge(c_ve.h, G[("qev", h, th)])
                                prev = (h, th - 1) if th else (h - 1, 1)
                                if h + th >= 1:
                                    t.wait_ge(c_ve.h, G[("rdev",) + prev])
                            if (h, th, mp) in G_scfree:
                                t.wait_ge(c_ve.h, G_scfree[(h, th, mp)])
                            for half in range(2):
                                mt = mp * 2 + half
                                t.matmul(
                                    sc_ps[(mp % 2) * 2 + half][:],
                                    kT_sb[hp][:, mt * 128:(mt + 1) * 128],
                                    q_bf[hp][:, th * TH:(th + 1) * TH],
                                    start=True, stop=True)
                            c_tb.inc(t.drain())

                        blk.tensor(tb)
                        gate = c_tb.n

                        def ve(v, h=h, th=th, mp=mp, gate=gate):
                            v.wait_ge(c_tb.h, gate)
                            for half in range(2):
                                mt = mp * 2 + half
                                ins = v.tensor_copy(
                                    scbuf[:, mt * TH:(mt + 1) * TH],
                                    sc_ps[(mp % 2) * 2 + half][:])
                            c_ve.inc(ins)
                            if mp + 2 <= 3:
                                G_scfree[(h, th, mp + 2)] = c_ve.n
                            else:
                                G_scfree[(h, th + 1, mp - 2) if th == 0
                                         else (h + 1, 0, mp - 2)] = c_ve.n

                        blk.vector(ve)
                    sc_all = c_ve.n

                    def sc(s, h=h, th=th, sc_all=sc_all):
                        s.wait_ge(c_ve.h, sc_all)
                        c_sc.inc(s.activation(eT_sb[:], scbuf[:], AF.Exp))
                        G[("exp", h, th)] = c_sc.n

                    blk.scalar(sc)

                def score_denao(h, th):
                    hp = h % 2

                    def tb(t, h=h, th=th, hp=hp):
                        t.wait_ge(c_sc.h, G[("exp", h, th)])
                        for mt in range(NMT):
                            t.matmul(sc_ps[0][0:1, :], ones_col[:],
                                     eT_sb[:, mt * TH:(mt + 1) * TH],
                                     start=(mt == 0), stop=(mt == NMT - 1))
                        prev = (h, th - 1) if th else (h - 1, 1)
                        if h + th >= 1:
                            t.wait_ge(c_ve.h, G[("afv",) + prev])
                        for mt in range(NMT):
                            ins = t.matmul(ao_ps[:],
                                           v_sb[hp][:, mt * 128:(mt + 1) * 128],
                                           eT_sb[:, mt * TH:(mt + 1) * TH],
                                           start=(mt == 0), stop=(mt == NMT - 1))
                        c_tb.inc(ins)

                    blk.tensor(tb)
                    gate = c_tb.n

                    def ve(v, h=h, th=th, gate=gate):
                        v.wait_ge(c_tb.h, gate)
                        v.reciprocal(rden_f[0:1, :], sc_ps[0][0:1, :])
                        _vg(v)
                        v.tensor_copy(den_bf[:], rden_f[0:1, :])
                        c_ve.inc(v.drain())

                    blk.vector(ve)
                    G[("denv", h, th)] = c_ve.n

                def score_rden(h, th):
                    def tb(t, h=h, th=th):
                        t.wait_ge(c_ve.h, G[("denv", h, th)])
                        t.matmul(sc_ps[1][:], ones_row[:], den_bf[:],
                                 start=True, stop=True)
                        c_tb.inc(t.drain())

                    blk.tensor(tb)
                    gate = c_tb.n

                    def ve(v, h=h, th=th, gate=gate):
                        v.wait_ge(c_tb.h, gate)
                        v.tensor_copy(rden_f[:], sc_ps[1][:])
                        _vg(v)
                        c_ve.inc(v.tensor_tensor(
                            afT[:, h * T + th * TH:h * T + th * TH + TH],
                            ao_ps[:], rden_f[:], OP.mult))

                    blk.vector(ve)
                    G[("rdev", h, th)] = c_ve.n
                    G[("afv", h, th)] = c_ve.n

                prep_pat(0)
                prep_kT(0)
                prep_v(0)
                prep_q(0)
                for h in range(H):
                    nxt = h + 1
                    score_scexp(h, 0)
                    if nxt < H:
                        prep_pat(nxt)
                    score_denao(h, 0)
                    if nxt < H:
                        prep_kT(nxt)
                    score_rden(h, 0)
                    score_scexp(h, 1)
                    if nxt < H:
                        prep_v(nxt)
                    score_denao(h, 1)
                    if nxt < H:
                        prep_q(nxt)
                    score_rden(h, 1)

                af_done = c_ve.n

            a12ctx.pop_all().close()

            with contextlib.ExitStack() as ectx:
                ECp = ectx.enter_context
                s_bf = ECp(nc.sbuf_tensor("s_bf", [128, KT * T], BF16))
                rr_f = ECp(nc.sbuf_tensor("rr_f", [1, T], F32))
                rr_bf = ECp(nc.sbuf_tensor("rr_bf", [1, T], BF16))
                rrB = [ECp(nc.sbuf_tensor(f"rrB{i}", [128, TH], F32))
                       for i in range(2)]
                t1_f = ECp(nc.sbuf_tensor("t1_f", [128, TH], F32))
                ss_ps = ECp(nc.psum_tensor("ss_ps", [1, T], F32))
                rrb_ps = [ECp(nc.psum_tensor(f"rrb_ps{i}", [128, TH], F32))
                          for i in range(2)]
                blk = ECp(nc.Block("A3"))

                ssd_list = []
                for kt in range(KT):
                    def a_sc_sq(s, kt=kt):
                        if kt == 0:
                            s.wait_ge(c_ve.h, af_done)
                        else:
                            s.wait_ge(c_tb.h, ssd_list[kt - 1])
                        c_sc.inc(s.activation(sqbuf[:], afT[:, kt * T:(kt + 1) * T],
                                              AF.Square))

                    blk.scalar(a_sc_sq)
                    sqd = c_sc.n

                    def a_tb_ss(t, kt=kt, sqd=sqd):
                        t.wait_ge(c_sc.h, sqd)
                        for th in range(2):
                            t.matmul(ss_ps[0:1, th * TH:(th + 1) * TH],
                                     ones_col[:],
                                     sqbuf[:, th * TH:(th + 1) * TH],
                                     start=(kt == 0), stop=(kt == KT - 1))
                        c_tb.inc(t.drain())

                    blk.tensor(a_tb_ss)
                    ssd_list.append(c_tb.n)
                ssd = ssd_list[-1]

                def a_ve_rr(v, ssd=ssd):
                    v.wait_ge(c_tb.h, ssd)
                    c_ve.inc(v.tensor_scalar(rr_f[:], ss_ps[0:1, 0:T], 1.0 / E,
                                             1e-6, OP.mult, OP.add))

                blk.vector(a_ve_rr)
                rr1 = c_ve.n

                def a_sc_rr(s, rr1=rr1):
                    s.wait_ge(c_ve.h, rr1)
                    c_sc.inc(s.sqrt(rr_f[:], rr_f[:]))

                blk.scalar(a_sc_rr)
                rr2 = c_sc.n

                def a_ve_rr2(v, rr2=rr2):
                    v.wait_ge(c_sc.h, rr2)
                    v.reciprocal(rr_f[:], rr_f[:])
                    _vg(v)
                    v.tensor_copy(rr_bf[:], rr_f[:])
                    c_ve.inc(v.drain())

                blk.vector(a_ve_rr2)
                rr3 = c_ve.n

                def a_tb_rrB(t, rr3=rr3):
                    t.wait_ge(c_ve.h, rr3)
                    t.matmul(rrb_ps[0][:], ones_row[:], rr_bf[0:1, 0:TH],
                             start=True, stop=True)
                    t.matmul(rrb_ps[1][:], ones_row[:], rr_bf[0:1, TH:T],
                             start=True, stop=True)
                    c_tb.inc(t.drain())

                blk.tensor(a_tb_rrB)
                rr4 = c_tb.n

                def a_ve_rrB(v, rr4=rr4):
                    v.wait_ge(c_tb.h, rr4)
                    v.tensor_copy(rrB[0][:], rrb_ps[0][:])
                    v.tensor_copy(rrB[1][:], rrb_ps[1][:])
                    _vg(v)
                    last = None
                    for kt in range(KT):
                        for th in range(2):
                            sl = slice(kt * T + th * TH, kt * T + th * TH + TH)
                            v.tensor_tensor(t1_f[:], afT[:, sl], rrB[th][:],
                                            OP.mult)
                            _vg(v)
                            v.scalar_tensor_tensor(
                                s_bf[:, sl], t1_f[:], wnr_f[:, kt:kt + 1],
                                xnT[:, sl], OP.mult, OP.add)
                    c_ve.inc(v.drain())

                blk.vector(a_ve_rrB)
                s_done = c_ve.n

                ex_tile = [ECp(nc.sbuf_tensor(f"ex_tile{i}", [128, E], F32))
                           for i in range(2)]
                eg_tile = [ECp(nc.sbuf_tensor(f"eg_tile{i}", [128, E], BF16))
                           for i in range(2)]
                es_tok = [ECp(nc.sbuf_tensor(f"es_tok{i}", [128, E], BF16))
                          for i in range(2)]
                esg = ECp(nc.sbuf_tensor("esg", [128, E], F32))
                eout = [ECp(nc.sbuf_tensor(f"eout{i}", [128, E], F32))
                        for i in range(2)]
                se_ps = [ECp(nc.psum_tensor(f"se_ps{i}", [128, KT * 128], BF16))
                         for i in range(2)]


                ex_loaded = [0] * NTP
                eg_loaded = [0] * NTP
                etr = [0] * NTP
                esv = [0] * NTP
                eov = [0] * NTP

                def e_sy0(sy):
                    for tp in range(2):
                        sy.dma_start(
                            ex_tile[tp][:],
                            bass.AP(x_d.tensor, tp * 128 * E, [[E, 128], [1, E]]),
                        ).then_inc(s_xp[tp % 2].h, 16); s_xp[tp % 2].n += 16
                        sy.dma_start(
                            eg_tile[tp][:],
                            bass.AP(g_dram.tensor, tp * 128 * E,
                                    [[E, 128], [1, E]]),
                        ).then_inc(s_xp[tp % 2].h, 16); s_xp[tp % 2].n += 16
                        eg_loaded[tp] = s_xp[tp % 2].n

                blk.sync(e_sy0)

                for tp in range(NTP):
                    def e_tb_tr(t, tp=tp):
                        if tp == 0:
                            t.wait_ge(c_ve.h, s_done)
                        if tp >= 2:
                            t.wait_ge(c_ve.h, esv[tp - 2])
                        for kt in range(KT):
                            t.transpose(
                                se_ps[tp % 2][:, kt * 128:(kt + 1) * 128],
                                s_bf[:, kt * T + tp * 128:kt * T + tp * 128 + 128],
                                ident_b[:])
                        c_tb.inc(t.drain())

                    blk.tensor(e_tb_tr)
                    etr[tp] = c_tb.n

                    def e_ve(v, tp=tp):
                        v.wait_ge(c_tb.h, etr[tp])
                        if tp >= 2:
                            v.wait_ge(s_out.h, 16 * (tp - 1))
                        v.tensor_copy(es_tok[tp % 2][:], se_ps[tp % 2][:])
                        v.wait_ge(s_xp[tp % 2].h, eg_loaded[tp])
                        _vg(v)
                        v.tensor_tensor(esg[:], es_tok[tp % 2][:],
                                        eg_tile[tp % 2][:], OP.mult)
                        _vg(v)
                        v.tensor_tensor(eout[tp % 2][:], ex_tile[tp % 2][:],
                                        esg[:], OP.add)
                        c_ve.inc(v.drain())

                    blk.vector(e_ve)
                    esv[tp] = c_ve.n
                    eov[tp] = c_ve.n

                    def e_sy(sy, tp=tp):
                        sy.wait_ge(c_ve.h, eov[tp])
                        sy.dma_start(
                            bass.AP(out_d.tensor, tp * 128 * E, [[E, 128], [1, E]]),
                            eout[tp % 2][:],
                        ).then_inc(s_out.h, 16); s_out.n += 16
                        if tp + 2 < NTP:
                            sy.dma_start(
                                ex_tile[tp % 2][:],
                                bass.AP(x_d.tensor, (tp + 2) * 128 * E,
                                        [[E, 128], [1, E]]),
                            ).then_inc(s_xp[tp % 2].h, 16); s_xp[tp % 2].n += 16
                            sy.dma_start(
                                eg_tile[tp % 2][:],
                                bass.AP(g_dram.tensor, (tp + 2) * 128 * E,
                                        [[E, 128], [1, E]]),
                            ).then_inc(s_xp[tp % 2].h, 16); s_xp[tp % 2].n += 16
                            eg_loaded[tp + 2] = s_xp[tp % 2].n

                    blk.sync(e_sy)

                def e_fin(sy):
                    sy.wait_ge(s_out.h, s_out.n)

                blk.sync(e_fin)

    return nc


_NC_CACHE = {}


def kernel(**inputs):
    x = np.asarray(inputs["L0_attn_query_input"], dtype=np.float32)
    xf = np.ascontiguousarray(x.reshape(8192, E))
    base = {
        "w_in": np.ascontiguousarray(inputs["L0_attn_W_in"], dtype=np.float32),
        "w_q": np.ascontiguousarray(inputs["L0_attn_W_q"], dtype=np.float32),
        "w_k": np.ascontiguousarray(inputs["L0_attn_W_k"], dtype=np.float32),
        "w_v": np.ascontiguousarray(inputs["L0_attn_W_v"], dtype=np.float32),
        "patterns": np.ascontiguousarray(inputs["L0_attn_stored_patterns"],
                                         dtype=np.float32),
        "w_nq": np.ascontiguousarray(inputs["L0_attn_norm_query_w"],
                                     dtype=np.float32),
        "w_nr": np.ascontiguousarray(inputs["L0_attn_norm_retrieved_w"],
                                     dtype=np.float32),
        "beta": np.asarray(inputs["L0_attn_beta"], dtype=np.float32).reshape(1, 1),
        "w_r": np.ascontiguousarray(inputs["L0_ffn_Wr"], dtype=np.float32),
        "b_r": np.asarray(inputs["L0_ffn_br"], dtype=np.float32).reshape(1, NE),
        "w_g": np.ascontiguousarray(inputs["L0_ffn_Wg"], dtype=np.float32),
        "w_u": np.ascontiguousarray(inputs["L0_ffn_Wu"], dtype=np.float32),
        "w_d": np.ascontiguousarray(inputs["L0_ffn_Wd"], dtype=np.float32),
    }
    nc = build_program()
    in_maps = []
    for c in range(8):
        m = dict(base)
        m["x"] = np.ascontiguousarray(xf[c * T:(c + 1) * T])
        in_maps.append(m)
    res = run_bass_kernel_spmd(nc, in_maps, core_ids=list(range(8)))
    out = np.concatenate([res.results[c]["out"] for c in range(8)], axis=0)
    return out.reshape(4, 2048, E)


# version 16 (final)
# speedup vs baseline: 1.1433x; 1.1433x over previous
import contextlib
import numpy as np

import concourse.bass as bass
import concourse.mybir as mybir
from concourse.bass_utils import run_bass_kernel_spmd

F32 = mybir.dt.float32
BF16 = mybir.dt.bfloat16
AF = mybir.ActivationFunctionType
OP = mybir.AluOpType
AX = mybir.AxisListType

T = 1024
NTP = T // 128
E = 2048
KT = E // 128
DFF = 8192
NE = 4
H = 16
D = 128
M = 1024
NMT = M // 128
TH = 512
FC = 128
NCH = DFF // FC
FG = 2048
NKF = FG // 128
NGRP = DFF // FG
DCH = 512
NDCH = E // DCH


def _vg(v):
    v.drain()


class Sem:

    def __init__(self, h):
        self.h = h
        self.n = 0

    def inc(self, ins, k=1):
        ins.then_inc(self.h, k)
        self.n += k
        return self.n


def build_program():
    nc = bass.Bass("TRN2", target_bir_lowering=False)

    x_d = nc.dram_tensor("x", [T, E], F32, kind="ExternalInput").ap()
    win_d = nc.dram_tensor("w_in", [E, E], F32, kind="ExternalInput").ap()
    wq_d = nc.dram_tensor("w_q", [D, D], F32, kind="ExternalInput").ap()
    wk_d = nc.dram_tensor("w_k", [D, D], F32, kind="ExternalInput").ap()
    wv_d = nc.dram_tensor("w_v", [D, D], F32, kind="ExternalInput").ap()
    pat_d = nc.dram_tensor("patterns", [H, M, D], F32, kind="ExternalInput").ap()
    wnq_d = nc.dram_tensor("w_nq", [E], F32, kind="ExternalInput").ap()
    wnr_d = nc.dram_tensor("w_nr", [E], F32, kind="ExternalInput").ap()
    beta_d = nc.dram_tensor("beta", [1, 1], F32, kind="ExternalInput").ap()
    wr_d = nc.dram_tensor("w_r", [E, NE], F32, kind="ExternalInput").ap()
    br_d = nc.dram_tensor("b_r", [1, NE], F32, kind="ExternalInput").ap()
    wg_d = nc.dram_tensor("w_g", [NE, E, DFF], F32, kind="ExternalInput").ap()
    wu_d = nc.dram_tensor("w_u", [NE, E, DFF], F32, kind="ExternalInput").ap()
    wd_d = nc.dram_tensor("w_d", [NE, DFF, E], F32, kind="ExternalInput").ap()
    out_d = nc.dram_tensor("out", [T, E], F32, kind="ExternalOutput").ap()
    winbf_d = nc.dram_tensor("winbf", [E, E], BF16).ap()
    g_dram = nc.dram_tensor("g_dram", [T, E], BF16).ap()

    ctx = contextlib.ExitStack()
    EC = ctx.enter_context
    with ctx:
        xnT = EC(nc.sbuf_tensor("xnT", [128, KT * T], BF16))
        mask_tok = EC(nc.sbuf_tensor("mask_tok", [128, NTP * NE], F32))
        rT_bf = EC(nc.sbuf_tensor("rT_bf", [1, T], BF16))
        ident_f = EC(nc.sbuf_tensor("ident_f", [128, 128], F32))
        ident_b = EC(nc.sbuf_tensor("ident_b", [128, 128], BF16))
        ones_col = EC(nc.sbuf_tensor("ones_col", [128, 1], BF16))
        ones_row = EC(nc.sbuf_tensor("ones_row", [1, 128], BF16))
        wnq_f = EC(nc.sbuf_tensor("wnq_f", [128, KT], F32))
        wnr_f = EC(nc.sbuf_tensor("wnr_f", [128, KT], F32))
        beta_sb = EC(nc.sbuf_tensor("beta_sb", [1, 1], F32))
        sigb_sb = EC(nc.sbuf_tensor("sigb_sb", [1, 1], F32))

        s_xp = [Sem(EC(nc.semaphore(f"s_xp{i}"))) for i in range(2)]
        s_sm = Sem(EC(nc.semaphore("s_sm")))
        s_gp = Sem(EC(nc.semaphore("s_gp")))
        s_ck = Sem(EC(nc.semaphore("s_ck")))
        s_wg = [Sem(EC(nc.semaphore(f"s_wg{i}"))) for i in range(4)]
        s_wu = [Sem(EC(nc.semaphore(f"s_wu{i}"))) for i in range(4)]
        s_wd = [Sem(EC(nc.semaphore(f"s_wd{i}"))) for i in range(2)]
        s_pat = [Sem(EC(nc.semaphore(f"s_pat{i}"))) for i in range(2)]
        s_wt = Sem(EC(nc.semaphore("s_wt")))
        s_out = Sem(EC(nc.semaphore("s_out")))
        s_gs = Sem(EC(nc.semaphore("s_gs")))
        c_tb = Sem(EC(nc.semaphore("c_tb")))
        c_ve = Sem(EC(nc.semaphore("c_ve")))
        c_sc = Sem(EC(nc.semaphore("c_sc")))
        f_gu = Sem(EC(nc.semaphore("f_gu")))
        f_wd = Sem(EC(nc.semaphore("f_wd")))
        f_pat = Sem(EC(nc.semaphore("f_pat")))
        f_win = Sem(EC(nc.semaphore("f_win")))
        f_x = Sem(EC(nc.semaphore("f_x")))

        wctx = contextlib.ExitStack()
        WC = wctx.enter_context
        NWB = 4
        wg_buf = [WC(nc.sbuf_tensor(f"wg_buf{i}", [128, KT * FC], BF16))
                  for i in range(NWB)]
        wu_buf = [WC(nc.sbuf_tensor(f"wu_buf{i}", [128, KT * FC], BF16))
                  for i in range(NWB)]
        wd_buf = [WC(nc.sbuf_tensor(f"wd_buf{i}", [128, NKF * DCH], BF16))
                  for i in range(2)]

        gu_loaded = {}
        wd_loaded = {}

        def load_gu_chunk(g, i, e, grp, c):
            par = i % NWB
            if i >= NWB:
                g.wait_ge(f_gu.h, i - NWB + 1)
            fc0 = grp * FG + c * FC
            src_g = bass.AP(wg_d.tensor, e * E * DFF + fc0,
                            [[DFF, 128], [DFF * 128, KT], [1, FC]])
            dst_g = bass.AP(wg_buf[par], 0, [[KT * FC, 128], [FC, KT], [1, FC]])
            g.dma_start(dst_g, src_g).then_inc(s_wg[par].h, 16)
            s_wg[par].n += 16
            src_u = bass.AP(wu_d.tensor, e * E * DFF + fc0,
                            [[DFF, 128], [DFF * 128, KT], [1, FC]])
            dst_u = bass.AP(wu_buf[par], 0, [[KT * FC, 128], [FC, KT], [1, FC]])
            g.dma_start(dst_u, src_u).then_inc(s_wu[par].h, 16)
            s_wu[par].n += 16
            gu_loaded[i] = (s_wg[par].n, s_wu[par].n)

        def load_wd_piece(g, j, e, grp, dch):
            par = j % 2
            if j >= 2:
                g.wait_ge(f_wd.h, j - 1)
            srcp = bass.AP(wd_d.tensor,
                           e * DFF * E + (grp * FG) * E + dch * DCH,
                           [[E, 128], [E * 128, NKF], [1, DCH]])
            dstp = bass.AP(wd_buf[par], 0,
                           [[NKF * DCH, 128], [DCH, NKF], [1, DCH]])
            g.dma_start(dstp, srcp).then_inc(s_wd[par].h, 16)
            s_wd[par].n += 16
            wd_loaded[j] = s_wd[par].n

        with contextlib.ExitStack() as pctx:
            PC = pctx.enter_context
            x_tile = [PC(nc.sbuf_tensor(f"x_tile{i}", [128, E], F32)) for i in range(2)]
            xn_f = [PC(nc.sbuf_tensor(f"xn_f{i}", [128, E], F32)) for i in range(2)]
            sq_f = [PC(nc.sbuf_tensor(f"sq_f{i}", [128, E], F32)) for i in range(2)]
            strip = [PC(nc.sbuf_tensor(f"strip{i}", [128, E], F32)) for i in range(2)]
            wr_sb = PC(nc.sbuf_tensor("wr_sb", [128, KT * NE], F32))
            br_row = PC(nc.sbuf_tensor("br_row", [1, NE], F32))
            br_hi = PC(nc.sbuf_tensor("br_hi", [1, NE], BF16))
            br_hf = PC(nc.sbuf_tensor("br_hf", [1, NE], F32))
            br_lo = PC(nc.sbuf_tensor("br_lo", [1, NE], BF16))
            brB = PC(nc.sbuf_tensor("brB", [128, NE], F32))
            logit_tok = PC(nc.sbuf_tensor("logit_tok", [128, NTP * NE], F32))
            rT_f = PC(nc.sbuf_tensor("rT_f", [1, T], F32))
            r_tok = PC(nc.sbuf_tensor("r_tok", [128, NTP], F32))
            st_sum = PC(nc.sbuf_tensor("st_sum", [128, 1], F32))
            st_ss = PC(nc.sbuf_tensor("st_ss", [128, 1], F32))
            st_m = PC(nc.sbuf_tensor("st_m", [128, 1], F32))
            st_var = PC(nc.sbuf_tensor("st_var", [128, 1], F32))
            st_t = PC(nc.sbuf_tensor("st_t", [128, 1], F32))
            st_rstd = PC(nc.sbuf_tensor("st_rstd", [128, 1], F32))
            mx = PC(nc.sbuf_tensor("mx", [128, 1], F32))
            nmx = PC(nc.sbuf_tensor("nmx", [128, 1], F32))
            e4 = PC(nc.sbuf_tensor("e4", [128, NE], F32))
            s4 = PC(nc.sbuf_tensor("s4", [128, 1], F32))
            tw = PC(nc.sbuf_tensor("tw", [128, 1], F32))
            ohs = PC(nc.sbuf_tensor("ohs", [128, NE], F32))
            run = PC(nc.sbuf_tensor("run", [128, 1], F32))
            nrun = PC(nc.sbuf_tensor("nrun", [128, 1], F32))

            tp_ps = [PC(nc.psum_tensor(f"tp_ps{i}", [128, 512], F32)) for i in range(2)]
            lt_ps = PC(nc.psum_tensor("lt_ps", [128, NTP * NE], F32))
            rt_ps = PC(nc.psum_tensor("rt_ps", [1, T], F32))
            br_ps = PC(nc.psum_tensor("br_ps", [128, NE], F32))

            blk = PC(nc.Block("P"))

            x_loaded = [0] * NTP
            rattn_done = [0] * NTP
            winbf_cnt = [0]

            def p_sync0(sy):
                sy.dma_start(
                    bass.AP(wr_sb, 0, [[KT * NE, 128], [NE, KT], [1, NE]]),
                    bass.AP(wr_d.tensor, 0, [[NE, 128], [NE * 128, KT], [1, NE]]),
                    allow_slow_non_contiguous=True,
                ).then_inc(s_sm.h, 16); s_sm.n += 16
                sy.dma_start(br_row[:], br_d).then_inc(s_sm.h, 16); s_sm.n += 16
                sy.dma_start(
                    wnr_f[:], wnr_d.rearrange("(kt p) -> p kt", p=128),
                    allow_slow_non_contiguous=True,
                ).then_inc(s_sm.h, 16); s_sm.n += 16
                sy.dma_start(
                    wnq_f[:], wnq_d.rearrange("(kt p) -> p kt", p=128),
                    allow_slow_non_contiguous=True,
                ).then_inc(s_sm.h, 16); s_sm.n += 16
                sy.dma_start(beta_sb[:], beta_d).then_inc(s_sm.h, 16); s_sm.n += 16
                for tp in range(2):
                    sy.dma_start(
                        x_tile[tp][:],
                        bass.AP(x_d.tensor, tp * 128 * E, [[E, 128], [1, E]]),
                    ).then_inc(s_xp[tp % 2].h, 16); s_xp[tp % 2].n += 16
                    x_loaded[tp] = s_xp[tp % 2].n

            blk.sync(p_sync0)
            sm_all = s_sm.n

            def p_gp0(g):
                g.memset(ident_f[:], 0.0)
                g.drain()
                g.affine_select(out=ident_f[:], in_=ident_f[:],
                                compare_op=OP.not_equal, fill=1.0, base=0,
                                pattern=[[-1, 128]], channel_multiplier=1)
                g.memset(ident_b[:], 0.0)
                g.drain()
                g.affine_select(out=ident_b[:], in_=ident_b[:],
                                compare_op=OP.not_equal, fill=1.0, base=0,
                                pattern=[[-1, 128]], channel_multiplier=1)
                g.memset(ones_col[:], 1.0)
                g.memset(ones_row[:], 1.0)
                s_ck.inc(g.drain())
                for i in range(KT):
                    g.dma_start(
                        bass.AP(winbf_d.tensor, i * 128 * E, [[E, 128], [1, E]]),
                        bass.AP(win_d.tensor, i * 128 * E, [[E, 128], [1, E]]),
                    ).then_inc(s_gp.h, 16); s_gp.n += 16

            blk.gpsimd(p_gp0)
            consts_done = 1

            ev_done = {}
            for tp in range(NTP):
                par = tp % 2

                def p_sc_sq(s, tp=tp, par=par):
                    if tp == 0:
                        s.wait_ge(s_sm.h, sm_all)
                        s.activation(sigb_sb[:], beta_sb[:], AF.Sigmoid)
                    s.wait_ge(s_xp[tp % 2].h, x_loaded[tp])
                    if tp >= 2:
                        s.wait_ge(c_ve.h, ev_done[("ssred", tp - 2)])
                    c_sc.inc(s.activation(sq_f[par][:], x_tile[par][:], AF.Square))

                blk.scalar(p_sc_sq)
                sq_ready = c_sc.n

                def p_ve_stats(v, tp=tp, par=par):
                    v.wait_ge(s_xp[tp % 2].h, x_loaded[tp])
                    if tp >= 1:
                        v.wait_ge(c_sc.h, rattn_done[tp - 1])
                    v.tensor_reduce(st_sum[:], x_tile[par][:], AX.X, OP.add)
                    v.wait_ge(c_sc.h, sq_ready)
                    v.tensor_reduce(st_ss[:], sq_f[par][:], AX.X, OP.add)
                    c_ve.inc(v.drain())
                    _vg(v)
                    v.tensor_scalar(st_m[:], st_sum[:], 1.0 / E, None, OP.mult)
                    _vg(v)
                    v.tensor_tensor(st_t[:], st_m[:], st_m[:], OP.mult)
                    v.tensor_scalar(st_var[:], st_ss[:], 1.0 / E, None, OP.mult)
                    _vg(v)
                    v.tensor_tensor(st_var[:], st_var[:], st_t[:], OP.subtract)
                    _vg(v)
                    v.tensor_scalar(st_t[:], st_var[:], 1e-5, None, OP.add)
                    _vg(v)
                    v.reciprocal(st_t[:], st_t[:])
                    c_ve.inc(v.drain())

                blk.vector(p_ve_stats)
                ev_done[("ssred", tp)] = c_ve.n - 1
                rstd2_ready = c_ve.n

                def p_sc_rstd(s, tp=tp):
                    s.wait_ge(c_ve.h, rstd2_ready)
                    c_sc.inc(s.sqrt(st_rstd[:], st_t[:]))

                blk.scalar(p_sc_rstd)
                rstd_ready = c_sc.n

                def p_ve_xn(v, tp=tp, par=par):
                    v.wait_ge(c_sc.h, rstd_ready)
                    if tp >= 2:
                        v.wait_ge(c_tb.h, ev_done[("trq", tp - 2, 3)])
                    f_x.inc(v.tensor_scalar(xn_f[par][:], x_tile[par][:], st_m[:],
                                            st_rstd[:], OP.subtract, OP.mult))
                    v.tensor_tensor(st_t[:], st_t[:], st_var[:], OP.mult)
                    _vg(v)
                    v.tensor_scalar(st_t[:], st_t[:], 1e-6, None, OP.add)
                    _vg(v)
                    v.reciprocal(st_t[:], st_t[:])
                    c_ve.inc(v.drain())

                blk.vector(p_ve_xn)
                xn_ready = c_ve.n
                x_free = f_x.n

                def p_sc_rattn(s, tp=tp):
                    s.wait_ge(c_ve.h, xn_ready)
                    c_sc.inc(s.sqrt(r_tok[:, tp:tp + 1], st_t[:]))

                blk.scalar(p_sc_rattn)
                rattn_ready = c_sc.n
                rattn_done[tp] = rattn_ready
                ev_done[("rattn", tp)] = rattn_ready

                def p_sy_next(sy, tp=tp):
                    sy.wait_ge(f_x.h, x_free)
                    if tp + 2 < NTP:
                        sy.dma_start(
                            x_tile[tp % 2][:],
                            bass.AP(x_d.tensor, (tp + 2) * 128 * E,
                                    [[E, 128], [1, E]]),
                        ).then_inc(s_xp[tp % 2].h, 16); s_xp[tp % 2].n += 16
                        x_loaded[tp + 2] = s_xp[tp % 2].n

                if tp + 2 < NTP:
                    blk.sync(p_sy_next)

                for q in range(4):
                    def p_tb_trq(t, tp=tp, par=par, q=q):
                        if tp == 0 and q == 0:
                            t.wait_ge(s_gp.h, consts_done)
                        t.wait_ge(c_ve.h, xn_ready)
                        qi = tp * 4 + q
                        if qi >= 2:
                            t.wait_ge(c_ve.h, ev_done[("evq", qi - 2)])
                        ps = tp_ps[qi % 2]
                        for j in range(4):
                            kt = q * 4 + j
                            t.transpose(ps[:, j * 128:(j + 1) * 128],
                                        xn_f[par][:, kt * 128:(kt + 1) * 128],
                                        ident_f[:])
                        c_tb.inc(t.drain())

                    blk.tensor(p_tb_trq)
                    ev_done[("trq", tp, q)] = c_tb.n

                    def p_ve_evq(v, tp=tp, q=q):
                        qi = tp * 4 + q
                        v.wait_ge(c_tb.h, ev_done[("trq", tp, q)])
                        if tp >= 2 and q == 0:
                            v.wait_ge(c_tb.h, ev_done[("rmm", tp - 2)])
                        ps = tp_ps[qi % 2]
                        for j in range(4):
                            kt = q * 4 + j
                            v.tensor_copy(
                                bass.AP(xnT, kt * T + tp * 128,
                                        [[KT * T, 128], [1, 128]]),
                                ps[:, j * 128:(j + 1) * 128])
                        c_ve.inc(v.tensor_copy(
                            strip[tp % 2][:, q * 512:(q + 1) * 512], ps[:]))

                    blk.vector(p_ve_evq)
                    ev_done[("evq", tp * 4 + q)] = c_ve.n

                def p_tb_router(t, tp=tp):
                    t.wait_ge(c_ve.h, ev_done[("evq", tp * 4 + 3)])
                    t.wait_ge(s_sm.h, sm_all)
                    for kt in range(KT):
                        t.matmul(lt_ps[:, tp * NE:(tp + 1) * NE],
                                 strip[tp % 2][:, kt * 128:(kt + 1) * 128],
                                 wr_sb[:, kt * NE:(kt + 1) * NE],
                                 start=(kt == 0), stop=(kt == KT - 1))
                    t.wait_ge(c_sc.h, ev_done[("rattn", tp)])
                    t.transpose(rt_ps[:, tp * 128:(tp + 1) * 128],
                                r_tok[:, tp:tp + 1], ident_f[:])
                    c_tb.inc(t.drain())

                blk.tensor(p_tb_router)
                ev_done[("rmm", tp)] = c_tb.n

            def p_ve_br(v):
                v.wait_ge(s_sm.h, sm_all)
                v.tensor_copy(br_hi[:], br_row[:])
                _vg(v)
                v.tensor_copy(br_hf[:], br_hi[:])
                _vg(v)
                v.tensor_tensor(br_lo[:], br_row[:], br_hf[:], OP.subtract)
                c_ve.inc(v.drain())

            blk.vector(p_ve_br)
            br_split = c_ve.n

            def p_tb_br(t):
                t.wait_ge(c_ve.h, br_split)
                t.matmul(br_ps[:], ones_row[:], br_hi[:], start=True, stop=False)
                t.matmul(br_ps[:], ones_row[:], br_lo[:], start=False, stop=True)
                c_tb.inc(t.drain())

            blk.tensor(p_tb_br)
            brB_done = c_tb.n

            def p_ve_fin(v):
                v.wait_ge(c_tb.h, brB_done)
                v.tensor_copy(brB[:], br_ps[:])
                v.tensor_copy(rT_f[:], rt_ps[:])
                _vg(v)
                sb_b = bass.AP(sigb_sb, 0, [[1, 1], [0, T]])
                v.tensor_tensor(rT_f[:], rT_f[:], sb_b, OP.mult)
                _vg(v)
                v.tensor_copy(rT_bf[:], rT_f[:])
                _vg(v)
                for tp in range(NTP):
                    brB_b = bass.AP(brB, 0, [[NE, 128], [1, NE]])
                    v.tensor_tensor(logit_tok[:, tp * NE:(tp + 1) * NE],
                                    lt_ps[:, tp * NE:(tp + 1) * NE],
                                    brB_b, OP.add)
                c_ve.inc(v.drain())

            blk.vector(p_ve_fin)
            logits_done = c_ve.n

            for tp in range(NTP):
                def p_ve_m1(v, tp=tp):
                    if tp == 0:
                        v.wait_ge(c_ve.h, logits_done)
                    L = logit_tok[:, tp * NE:(tp + 1) * NE]
                    v.tensor_reduce(mx[:], L, AX.X, OP.max)
                    _vg(v)
                    v.tensor_scalar(nmx[:], mx[:], -1.0, None, OP.mult)
                    c_ve.inc(v.drain())

                blk.vector(p_ve_m1)
                m1 = c_ve.n

                def p_sc_exp(s, tp=tp):
                    s.wait_ge(c_ve.h, m1)
                    c_sc.inc(s.activation(e4[:], logit_tok[:, tp * NE:(tp + 1) * NE],
                                          AF.Exp, bias=nmx[:], scale=1.0))

                blk.scalar(p_sc_exp)
                expd = c_sc.n

                def p_ve_m2(v, tp=tp):
                    L = logit_tok[:, tp * NE:(tp + 1) * NE]
                    v.wait_ge(c_sc.h, expd)
                    v.tensor_reduce(s4[:], e4[:], AX.X, OP.add)
                    _vg(v)
                    v.reciprocal(tw[:], s4[:])
                    for e in range(NE):
                        v.tensor_tensor(ohs[:, e:e + 1], L[:, e:e + 1], mx[:],
                                        OP.is_ge)
                    _vg(v)
                    v.tensor_copy(run[:], ohs[:, 0:1])
                    _vg(v)
                    for e in range(1, NE):
                        v.tensor_scalar(nrun[:], run[:], -1.0, 1.0, OP.mult, OP.add)
                        _vg(v)
                        v.tensor_tensor(ohs[:, e:e + 1], ohs[:, e:e + 1], nrun[:],
                                        OP.mult)
                        _vg(v)
                        if e < NE - 1:
                            v.tensor_tensor(run[:], run[:], ohs[:, e:e + 1], OP.max)
                            _vg(v)
                    for e in range(NE):
                        v.tensor_tensor(
                            mask_tok[:, tp * NE + e:tp * NE + e + 1],
                            ohs[:, e:e + 1], tw[:], OP.mult)
                    c_ve.inc(v.drain())

                blk.vector(p_ve_m2)

        with contextlib.ExitStack() as fctx:
            FC_ = fctx.enter_context
            h_buf = FC_(nc.sbuf_tensor("h_buf", [128, NKF * T], BF16))
            y_sb = FC_(nc.sbuf_tensor("y_sb", [128, NTP * E], F32))
            sg_buf = [FC_(nc.sbuf_tensor(f"sg_buf{i}", [128, 512], F32))
                      for i in range(3)]
            g_stage = FC_(nc.sbuf_tensor("g_stage", [128, E], BF16))

            pg = [FC_(nc.psum_tensor(f"pg{i}", [128, 512], F32)) for i in range(3)]
            pu = [FC_(nc.psum_tensor(f"pu{i}", [128, 512], F32)) for i in range(3)]

            blk = FC_(nc.Block("F"))

            seq_inc = {}
            seq_free = {}
            gchunk = 0
            gpiece = 0
            seq = 0

            groups = [(e, grp) for e in range(NE) for grp in range(NGRP)]

            for egi, (e, grp) in enumerate(groups):
                if True:
                    def f_gp_load(g, egi=egi, e=e, grp=grp, gchunk0=gchunk,
                                  gpiece0=gpiece):
                        if egi == 0:
                            for c in range(FG // FC):
                                load_gu_chunk(g, gchunk0 + c, e, grp, c)
                        for dch in range(E // DCH):
                            load_wd_piece(g, gpiece0 + dch, e, grp, dch)

                    blk.gpsimd(f_gp_load)
                    grp_groups = []
                    for c in range(FG // FC):
                        for fp in range(FC // 128):
                            for th in range(2):
                                grp_groups.append((seq, gchunk + c, c, fp, th))
                                seq += 1
                    pend_evac = []

                    def emit_gu_group(gi_tuple):
                        sq, ichunk, c, fp, th = gi_tuple
                        bank = sq % 3
                        cpar = ichunk % NWB

                        def f_tb(t, sq=sq, ichunk=ichunk, c=c, fp=fp, th=th,
                                 bank=bank, cpar=cpar):
                            if fp == 0 and th == 0:
                                t.wait_ge(s_wg[cpar].h, gu_loaded[ichunk][0])
                                t.wait_ge(s_wu[cpar].h, gu_loaded[ichunk][1])
                            if sq - 3 in seq_free:
                                t.wait_ge(c_ve.h, seq_free[sq - 3])
                            for kt in range(KT):
                                t.matmul(
                                    pg[bank][:],
                                    wg_buf[cpar][:, kt * FC + fp * 128:
                                                 kt * FC + fp * 128 + 128],
                                    xnT[:, kt * T + th * TH:
                                        kt * T + th * TH + TH],
                                    start=(kt == 0), stop=(kt == KT - 1))
                            for kt in range(KT):
                                ins = t.matmul(
                                    pu[bank][:],
                                    wu_buf[cpar][:, kt * FC + fp * 128:
                                                 kt * FC + fp * 128 + 128],
                                    xnT[:, kt * T + th * TH:
                                        kt * T + th * TH + TH],
                                    start=(kt == 0), stop=(kt == KT - 1))
                                if kt == KT - 2 and fp == 0 and th == 1 and c >= 1:
                                    f_gu.inc(ins)
                                if kt == KT - 1:
                                    c_tb.inc(ins)

                        blk.tensor(f_tb)
                        seq_inc[sq] = c_tb.n

                    def emit_gu_evac(gi_tuple, gate):
                        sq, ichunk, c, fp, th, = gi_tuple
                        bank = sq % 3

                        def f_sc(s, sq=sq, bank=bank, gate=gate):
                            s.wait_ge(c_tb.h, gate)
                            c_sc.inc(s.activation(sg_buf[bank][:], pg[bank][:],
                                                  AF.Silu))

                        blk.scalar(f_sc)
                        silu_done = c_sc.n

                        def f_ve(v, sq=sq, c=c, fp=fp, th=th, bank=bank,
                                 silu_done=silu_done):
                            v.wait_ge(c_sc.h, silu_done)
                            kf = c * (FC // 128) + fp
                            c_ve.inc(v.tensor_tensor(
                                h_buf[:, kf * T + th * TH:
                                      kf * T + th * TH + TH],
                                sg_buf[bank][:], pu[bank][:], OP.mult))

                        blk.vector(f_ve)
                        seq_free[sq] = c_ve.n

                    for gi_tuple in grp_groups:
                        emit_gu_group(gi_tuple)
                        pend_evac.append(gi_tuple)
                        if len(pend_evac) >= 2:
                            g0 = pend_evac.pop(0)
                            emit_gu_evac(g0, seq_inc[g0[0] + 1])

                    def f_tb_gu_end(t):
                        c_tb.inc(t.drain())

                    blk.tensor(f_tb_gu_end)
                    g0 = pend_evac.pop(0)
                    emit_gu_evac(g0, c_tb.n)
                    h_ready_val = c_ve.n

                    y_groups = []
                    for dch in range(E // DCH):
                        for tp2 in range(NTP):
                            y_groups.append((seq, gpiece + dch, dch, tp2))
                            seq += 1
                    pend_y = []
                    first_y_seq = y_groups[0][0]

                    def emit_y_group(gt):
                        sq, jpiece, dch, tp2 = gt
                        bank = sq % 3
                        dpar = jpiece % 2

                        def f_tb(t, sq=sq, jpiece=jpiece, dch=dch, tp2=tp2,
                                 bank=bank, dpar=dpar):
                            if tp2 == 0:
                                t.wait_ge(s_wd[dpar].h, wd_loaded[jpiece])
                            if sq == first_y_seq:
                                t.wait_ge(c_ve.h, h_ready_val)
                            if sq - 3 in seq_free:
                                t.wait_ge(c_ve.h, seq_free[sq - 3])
                            for kf in range(NKF):
                                ins = t.matmul(
                                    pg[bank][:],
                                    h_buf[:, kf * T + tp2 * 128:
                                          kf * T + tp2 * 128 + 128],
                                    wd_buf[dpar][:, kf * DCH:(kf + 1) * DCH],
                                    start=(kf == 0), stop=(kf == NKF - 1))
                                if kf == NKF - 2 and tp2 == 1 and dch >= 1:
                                    f_wd.inc(ins)
                                if kf == NKF - 2 and tp2 == 1 and dch == 0:
                                    f_gu.inc(ins)
                                if kf == NKF - 1:
                                    c_tb.inc(ins)

                        blk.tensor(f_tb)
                        seq_inc[sq] = c_tb.n

                    def emit_y_evac(gt, gate, e=e, grp=grp):
                        sq, jpiece, dch, tp2 = gt
                        bank = sq % 3

                        def f_ve(v, sq=sq, dch=dch, tp2=tp2, bank=bank,
                                 gate=gate):
                            v.wait_ge(c_tb.h, gate)
                            ysl = y_sb[:, tp2 * E + dch * DCH:
                                       tp2 * E + (dch + 1) * DCH]
                            msl = mask_tok[:, tp2 * NE + e:tp2 * NE + e + 1]
                            if e == 0 and grp == 0:
                                c_ve.inc(v.tensor_scalar(ysl, pg[bank][:],
                                                         msl, None, OP.mult))
                            else:
                                c_ve.inc(v.scalar_tensor_tensor(
                                    ysl, pg[bank][:], msl, ysl,
                                    OP.mult, OP.add))

                        blk.vector(f_ve)
                        seq_free[sq] = c_ve.n

                    def f_gp_next(g, egi=egi, gchunk0=gchunk):
                        if egi + 1 < len(groups):
                            e2, grp2 = groups[egi + 1]
                            nxt0 = gchunk0 + FG // FC
                            for c in range(FG // FC):
                                load_gu_chunk(g, nxt0 + c, e2, grp2, c)

                    blk.gpsimd(f_gp_next)

                    for gt in y_groups:
                        emit_y_group(gt)
                        pend_y.append(gt)
                        if len(pend_y) >= 2:
                            g0 = pend_y.pop(0)
                            emit_y_evac(g0, seq_inc[g0[0] + 1])

                    def f_tb_end(t):
                        c_tb.inc(t.drain())
                        t.nop().then_inc(f_wd.h, 1); f_wd.n += 1

                    blk.tensor(f_tb_end)
                    end_gate = c_tb.n
                    g0 = pend_y.pop(0)
                    emit_y_evac(g0, end_gate)

                    gchunk += FG // FC
                    gpiece += E // DCH

            y_done_v = c_ve.n

            def f_sc_g(s):
                s.wait_ge(c_ve.h, y_done_v)
                for tp2 in range(NTP):
                    if tp2 >= 1:
                        s.wait_ge(s_gs.h, 16 * tp2)
                    for cc in range(4):
                        ins = s.activation(
                            g_stage[:, cc * 512:(cc + 1) * 512],
                            y_sb[:, tp2 * E + cc * 512:tp2 * E + (cc + 1) * 512],
                            AF.Silu)
                        if cc == 3:
                            c_sc.inc(ins)
                    s.wait_ge(c_sc.h, c_sc.n)
                    s.dma_start(
                        bass.AP(g_dram.tensor, tp2 * 128 * E, [[E, 128], [1, E]]),
                        g_stage[:],
                    ).then_inc(s_gs.h, 16); s_gs.n += 16
                s.wait_ge(s_gs.h, s_gs.n)

            blk.scalar(f_sc_g)

        wctx.pop_all().close()

        with contextlib.ExitStack() as actx:
            AC = actx.enter_context
            afT = AC(nc.sbuf_tensor("afT", [128, KT * T], BF16))
            sqbuf = AC(nc.sbuf_tensor("sqbuf", [128, T], BF16))
            a12ctx = contextlib.ExitStack()
            AC = a12ctx.enter_context
            projT = AC(nc.sbuf_tensor("projT", [128, KT * T], BF16))
            rB_sb = [AC(nc.sbuf_tensor(f"rB_sb{i}", [128, TH], F32))
                     for i in range(2)]
            wq_bf = AC(nc.sbuf_tensor("wq_bf", [128, 128], BF16))
            wk_bf = AC(nc.sbuf_tensor("wk_bf", [128, 128], BF16))
            wv_bf = AC(nc.sbuf_tensor("wv_bf", [128, 128], BF16))
            wqT = AC(nc.sbuf_tensor("wqT", [128, 128], BF16))
            wkT = AC(nc.sbuf_tensor("wkT", [128, 128], BF16))
            wvT = AC(nc.sbuf_tensor("wvT", [128, 128], BF16))
            pat_bf = [AC(nc.sbuf_tensor(f"pat_bf{i}", [128, NMT * 128], BF16))
                      for i in range(2)]
            patT_sb = [AC(nc.sbuf_tensor(f"patT_sb{i}", [128, NMT * 128], BF16))
                       for i in range(2)]
            kT_sb = [AC(nc.sbuf_tensor(f"kT_sb{i}", [128, M], BF16))
                     for i in range(2)]
            v_sb = [AC(nc.sbuf_tensor(f"v_sb{i}", [128, NMT * 128], BF16))
                    for i in range(2)]
            q_bf = [AC(nc.sbuf_tensor(f"q_bf{i}", [128, M], BF16))
                    for i in range(2)]
            eT_sb = AC(nc.sbuf_tensor("eT_sb", [128, NMT * TH], BF16))
            scbuf = AC(nc.sbuf_tensor("scbuf", [128, NMT * TH], BF16))
            den_bf = AC(nc.sbuf_tensor("den_bf", [1, TH], BF16))
            rden_f = AC(nc.sbuf_tensor("rden_f", [128, TH], F32))
            win_buf = [AC(nc.sbuf_tensor(f"win_buf{i}", [128, KT * 512], BF16))
                       for i in range(2)]

            with contextlib.ExitStack() as a0ctx:
                A0 = a0ctx.enter_context
                pj_ps = [A0(nc.psum_tensor(f"pj_ps{i}", [128, 512], F32))
                         for i in range(2)]
                small_ps = A0(nc.psum_tensor("small_ps", [128, 512], F32))
                smallb_ps = A0(nc.psum_tensor("smallb_ps", [128, 384], BF16))

                blk = A0(nc.Block("A0"))

                def a_gp0(g):
                    g.dma_start(wq_bf[:], wq_d).then_inc(s_gp.h, 16); s_gp.n += 16
                    g.dma_start(wk_bf[:], wk_d).then_inc(s_gp.h, 16); s_gp.n += 16
                    g.dma_start(wv_bf[:], wv_d).then_inc(s_gp.h, 16); s_gp.n += 16
                    for h in range(2):
                        src = bass.AP(pat_d.tensor, h * M * D,
                                      [[D, 128], [D * 128, NMT], [1, D]])
                        dst = bass.AP(pat_bf[h], 0,
                                      [[NMT * 128, 128], [128, NMT], [1, D]])
                        g.dma_start(dst, src).then_inc(s_pat[h].h, 16)
                        s_pat[h].n += 16

                blk.gpsimd(a_gp0)
                smalls = s_gp.n

                def a_tb0(t):
                    t.wait_ge(s_gp.h, smalls)
                    t.transpose(smallb_ps[:, 0:128], wq_bf[:], ident_b[:])
                    t.transpose(smallb_ps[:, 128:256], wk_bf[:], ident_b[:])
                    t.transpose(smallb_ps[:, 256:384], wv_bf[:], ident_b[:])
                    t.matmul(pj_ps[0][:], ones_row[:], rT_bf[0:1, 0:TH],
                             start=True, stop=True)
                    t.matmul(pj_ps[1][:], ones_row[:], rT_bf[0:1, TH:T],
                             start=True, stop=True)
                    c_tb.inc(t.drain())

                blk.tensor(a_tb0)
                a0t = c_tb.n

                def a_ve0(v):
                    v.wait_ge(c_tb.h, a0t)
                    v.tensor_copy(wqT[:], smallb_ps[:, 0:128])
                    v.tensor_copy(wkT[:], smallb_ps[:, 128:256])
                    v.tensor_copy(wvT[:], smallb_ps[:, 256:384])
                    v.tensor_copy(rB_sb[0][:], pj_ps[0][:])
                    v.tensor_copy(rB_sb[1][:], pj_ps[1][:])
                    c_ve.inc(v.drain())

                blk.vector(a_ve0)
                a0v = c_ve.n

                win_loaded = [0] * 4
                pj_free = {}
                wqfold = [0] * 4
                pji = 0
                for och in range(4):
                    def a_sy_win2(sy, och=och):
                        if och == 0:
                            sy.wait_ge(s_gp.h, winbf_cnt[0])
                        if och >= 2:
                            sy.wait_ge(f_win.h, och - 1)
                        for kt in range(KT):
                            src = bass.AP(
                                winbf_d.tensor, (och * 512) * E + kt * 128,
                                [[E, 512], [1, 128]])
                            sy.dma_start_transpose(
                                win_buf[och % 2][:, kt * 512:(kt + 1) * 512],
                                src).then_inc(s_wt.h, 16)
                            s_wt.n += 16

                    blk.sync(a_sy_win2)
                    win_loaded[och] = s_wt.n

                    def a_ve_fold(v, och=och):
                        v.wait_ge(s_wt.h, win_loaded[och])
                        for kt in range(KT):
                            v.tensor_scalar(
                                win_buf[och % 2][:, kt * 512:(kt + 1) * 512],
                                win_buf[och % 2][:, kt * 512:(kt + 1) * 512],
                                wnq_f[:, kt:kt + 1], None, OP.mult)
                        c_ve.inc(v.drain())

                    blk.vector(a_ve_fold)
                    wqfold[och] = c_ve.n

                    for op in range(4):
                        for th in range(2):
                            cur = pji
                            pji += 1

                            def a_tb_pj(t, och=och, op=op, th=th, cur=cur):
                                if op == 0 and th == 0:
                                    t.wait_ge(c_ve.h, wqfold[och])
                                if cur < 2:
                                    t.wait_ge(c_ve.h, a0v)
                                else:
                                    t.wait_ge(c_ve.h, pj_free[cur - 2])
                                for kt in range(KT):
                                    t.matmul(
                                        pj_ps[cur % 2][:],
                                        win_buf[och % 2][:, kt * 512 + op * 128:
                                                         kt * 512 + op * 128 + 128],
                                        xnT[:, kt * T + th * TH:
                                            kt * T + th * TH + TH],
                                        start=(kt == 0), stop=(kt == KT - 1))
                                c_tb.inc(t.drain())
                                if op == 3 and th == 1:
                                    f_win.inc(t.nop())

                            blk.tensor(a_tb_pj)
                            pjd = c_tb.n

                            def a_ve_pj(v, och=och, op=op, th=th, cur=cur,
                                        pjd=pjd):
                                v.wait_ge(c_tb.h, pjd)
                                ot = och * 4 + op
                                c_ve.inc(v.tensor_tensor(
                                    projT[:, ot * T + th * TH:
                                          ot * T + th * TH + TH],
                                    pj_ps[cur % 2][:], rB_sb[th][:], OP.mult))

                            blk.vector(a_ve_pj)
                            pj_free[cur] = c_ve.n

            with contextlib.ExitStack() as a2ctx:
                A2 = a2ctx.enter_context
                patT_ps = A2(nc.psum_tensor("patT_ps", [128, NMT * 128], BF16))
                kv_ps = A2(nc.psum_tensor("kv_ps", [128, M], F32))
                sc_ps = [A2(nc.psum_tensor(f"sc_ps{i}", [128, TH], F32))
                         for i in range(4)]
                ao_ps = A2(nc.psum_tensor("ao_ps", [128, TH], F32))

                blk = A2(nc.Block("A2"))

                G = {}
                G_scfree = {}

                def prep_pat(h):
                    hp = h % 2

                    def tb(t, h=h, hp=hp):
                        t.wait_ge(s_pat[hp].h, 16 * (h // 2 + 1))
                        if h >= 1:
                            t.wait_ge(c_ve.h, G[("patv", h - 1)])
                        for mt in range(NMT):
                            t.transpose(patT_ps[:, mt * 128:(mt + 1) * 128],
                                        pat_bf[hp][:, mt * 128:(mt + 1) * 128],
                                        ident_b[:])
                        c_tb.inc(t.drain())
                        f_pat.inc(t.nop())

                    blk.tensor(tb)
                    gate = c_tb.n

                    def gp(g, h=h):
                        if h + 2 < H:
                            g.wait_ge(f_pat.h, h + 1)
                            srcp = bass.AP(pat_d.tensor, (h + 2) * M * D,
                                           [[D, 128], [D * 128, NMT], [1, D]])
                            dstp = bass.AP(pat_bf[h % 2], 0,
                                           [[NMT * 128, 128], [128, NMT], [1, D]])
                            g.dma_start(dstp, srcp).then_inc(s_pat[h % 2].h, 16)
                            s_pat[h % 2].n += 16

                    blk.gpsimd(gp)

                    def ve(v, h=h, hp=hp, gate=gate):
                        v.wait_ge(c_tb.h, gate)
                        v.tensor_copy(patT_sb[hp][:], patT_ps[:])
                        c_ve.inc(v.drain())

                    blk.vector(ve)
                    G[("patv", h)] = c_ve.n

                def prep_kT(h):
                    hp = h % 2

                    def tb(t, h=h, hp=hp):
                        t.wait_ge(c_ve.h, G[("patv", h)])
                        if h >= 1:
                            t.wait_ge(c_ve.h, G[("qev", h - 1, 1)])
                        t.matmul(kv_ps[:, 0:512], wkT[:],
                                 patT_sb[hp][:, 0:512], start=True, stop=True)
                        t.matmul(kv_ps[:, 512:M], wkT[:],
                                 patT_sb[hp][:, 512:M], start=True, stop=True)
                        c_tb.inc(t.drain())

                    blk.tensor(tb)
                    gate = c_tb.n

                    def ve(v, hp=hp, gate=gate):
                        v.wait_ge(c_tb.h, gate)
                        v.tensor_copy(kT_sb[hp][:], kv_ps[:])
                        c_ve.inc(v.drain())

                    blk.vector(ve)
                    G[("ktv", h)] = c_ve.n

                def prep_v(h):
                    hp = h % 2

                    def tb(t, h=h, hp=hp):
                        t.wait_ge(c_ve.h, G[("ktv", h)])
                        for mt in range(NMT):
                            t.matmul(kv_ps[:, mt * 128:(mt + 1) * 128],
                                     patT_sb[hp][:, mt * 128:(mt + 1) * 128],
                                     wvT[:], start=True, stop=True)
                        c_tb.inc(t.drain())

                    blk.tensor(tb)
                    gate = c_tb.n

                    def ve(v, hp=hp, gate=gate):
                        v.wait_ge(c_tb.h, gate)
                        v.tensor_copy(v_sb[hp][:], kv_ps[:])
                        c_ve.inc(v.drain())

                    blk.vector(ve)
                    G[("vv", h)] = c_ve.n

                def prep_q(h):
                    hp = h % 2

                    def tb(t, h=h, hp=hp):
                        t.wait_ge(c_ve.h, G[("vv", h)])
                        for th in range(2):
                            t.matmul(kv_ps[:, th * TH:(th + 1) * TH], wqT[:],
                                     projT[:, h * T + th * TH:
                                           h * T + th * TH + TH],
                                     start=True, stop=True)
                        c_tb.inc(t.drain())

                    blk.tensor(tb)
                    gate = c_tb.n

                    def ve(v, hp=hp, gate=gate):
                        v.wait_ge(c_tb.h, gate)
                        v.tensor_copy(q_bf[hp][:], kv_ps[:])
                        c_ve.inc(v.drain())

                    blk.vector(ve)
                    G[("qev", h, 0)] = c_ve.n
                    G[("qev", h, 1)] = c_ve.n

                def score_scexp(h, th):
                    hp = h % 2
                    for mp in range(4):
                        def tb(t, h=h, th=th, hp=hp, mp=mp):
                            if mp == 0:
                                t.wait_ge(c_ve.h, G[("qev", h, th)])
                                prev = (h, th - 1) if th else (h - 1, 1)
                                if h + th >= 1:
                                    t.wait_ge(c_ve.h, G[("rdev",) + prev])
                            if (h, th, mp) in G_scfree:
                                t.wait_ge(c_ve.h, G_scfree[(h, th, mp)])
                            for half in range(2):
                                mt = mp * 2 + half
                                t.matmul(
                                    sc_ps[(mp % 2) * 2 + half][:],
                                    kT_sb[hp][:, mt * 128:(mt + 1) * 128],
                                    q_bf[hp][:, th * TH:(th + 1) * TH],
                                    start=True, stop=True)
                            c_tb.inc(t.drain())

                        blk.tensor(tb)
                        gate = c_tb.n

                        def ve(v, h=h, th=th, mp=mp, gate=gate):
                            v.wait_ge(c_tb.h, gate)
                            for half in range(2):
                                mt = mp * 2 + half
                                ins = v.tensor_copy(
                                    scbuf[:, mt * TH:(mt + 1) * TH],
                                    sc_ps[(mp % 2) * 2 + half][:])
                            c_ve.inc(ins)
                            if mp + 2 <= 3:
                                G_scfree[(h, th, mp + 2)] = c_ve.n
                            else:
                                G_scfree[(h, th + 1, mp - 2) if th == 0
                                         else (h + 1, 0, mp - 2)] = c_ve.n

                        blk.vector(ve)
                    sc_all = c_ve.n

                    def sc(s, h=h, th=th, sc_all=sc_all):
                        s.wait_ge(c_ve.h, sc_all)
                        c_sc.inc(s.activation(eT_sb[:], scbuf[:], AF.Exp))
                        G[("exp", h, th)] = c_sc.n

                    blk.scalar(sc)

                def score_denao(h, th):
                    hp = h % 2

                    def tb(t, h=h, th=th, hp=hp):
                        t.wait_ge(c_sc.h, G[("exp", h, th)])
                        for mt in range(NMT):
                            t.matmul(sc_ps[0][0:1, :], ones_col[:],
                                     eT_sb[:, mt * TH:(mt + 1) * TH],
                                     start=(mt == 0), stop=(mt == NMT - 1))
                        prev = (h, th - 1) if th else (h - 1, 1)
                        if h + th >= 1:
                            t.wait_ge(c_ve.h, G[("afv",) + prev])
                        for mt in range(NMT):
                            ins = t.matmul(ao_ps[:],
                                           v_sb[hp][:, mt * 128:(mt + 1) * 128],
                                           eT_sb[:, mt * TH:(mt + 1) * TH],
                                           start=(mt == 0), stop=(mt == NMT - 1))
                        c_tb.inc(ins)

                    blk.tensor(tb)
                    gate = c_tb.n

                    def ve(v, h=h, th=th, gate=gate):
                        v.wait_ge(c_tb.h, gate)
                        v.reciprocal(rden_f[0:1, :], sc_ps[0][0:1, :])
                        _vg(v)
                        v.tensor_copy(den_bf[:], rden_f[0:1, :])
                        c_ve.inc(v.drain())

                    blk.vector(ve)
                    G[("denv", h, th)] = c_ve.n

                def score_rden(h, th):
                    def tb(t, h=h, th=th):
                        t.wait_ge(c_ve.h, G[("denv", h, th)])
                        t.matmul(sc_ps[1][:], ones_row[:], den_bf[:],
                                 start=True, stop=True)
                        c_tb.inc(t.drain())

                    blk.tensor(tb)
                    gate = c_tb.n

                    def ve(v, h=h, th=th, gate=gate):
                        v.wait_ge(c_tb.h, gate)
                        v.tensor_copy(rden_f[:], sc_ps[1][:])
                        _vg(v)
                        c_ve.inc(v.tensor_tensor(
                            afT[:, h * T + th * TH:h * T + th * TH + TH],
                            ao_ps[:], rden_f[:], OP.mult))

                    blk.vector(ve)
                    G[("rdev", h, th)] = c_ve.n
                    G[("afv", h, th)] = c_ve.n

                prep_pat(0)
                prep_kT(0)
                prep_v(0)
                prep_q(0)
                for h in range(H):
                    nxt = h + 1
                    score_scexp(h, 0)
                    if nxt < H:
                        prep_pat(nxt)
                    score_denao(h, 0)
                    if nxt < H:
                        prep_kT(nxt)
                    score_rden(h, 0)
                    score_scexp(h, 1)
                    if nxt < H:
                        prep_v(nxt)
                    score_denao(h, 1)
                    if nxt < H:
                        prep_q(nxt)
                    score_rden(h, 1)

                af_done = c_ve.n

            a12ctx.pop_all().close()

            with contextlib.ExitStack() as ectx:
                ECp = ectx.enter_context
                s_bf = ECp(nc.sbuf_tensor("s_bf", [128, KT * T], BF16))
                rr_f = ECp(nc.sbuf_tensor("rr_f", [1, T], F32))
                rr_bf = ECp(nc.sbuf_tensor("rr_bf", [1, T], BF16))
                rrB = [ECp(nc.sbuf_tensor(f"rrB{i}", [128, TH], F32))
                       for i in range(2)]
                t1_f = ECp(nc.sbuf_tensor("t1_f", [128, TH], F32))
                ss_ps = ECp(nc.psum_tensor("ss_ps", [1, T], F32))
                rrb_ps = [ECp(nc.psum_tensor(f"rrb_ps{i}", [128, TH], F32))
                          for i in range(2)]
                blk = ECp(nc.Block("A3"))

                ssd_list = []
                for kt in range(KT):
                    def a_sc_sq(s, kt=kt):
                        if kt == 0:
                            s.wait_ge(c_ve.h, af_done)
                        else:
                            s.wait_ge(c_tb.h, ssd_list[kt - 1])
                        c_sc.inc(s.activation(sqbuf[:], afT[:, kt * T:(kt + 1) * T],
                                              AF.Square))

                    blk.scalar(a_sc_sq)
                    sqd = c_sc.n

                    def a_tb_ss(t, kt=kt, sqd=sqd):
                        t.wait_ge(c_sc.h, sqd)
                        for th in range(2):
                            t.matmul(ss_ps[0:1, th * TH:(th + 1) * TH],
                                     ones_col[:],
                                     sqbuf[:, th * TH:(th + 1) * TH],
                                     start=(kt == 0), stop=(kt == KT - 1))
                        c_tb.inc(t.drain())

                    blk.tensor(a_tb_ss)
                    ssd_list.append(c_tb.n)
                ssd = ssd_list[-1]

                def a_ve_rr(v, ssd=ssd):
                    v.wait_ge(c_tb.h, ssd)
                    c_ve.inc(v.tensor_scalar(rr_f[:], ss_ps[0:1, 0:T], 1.0 / E,
                                             1e-6, OP.mult, OP.add))

                blk.vector(a_ve_rr)
                rr1 = c_ve.n

                def a_sc_rr(s, rr1=rr1):
                    s.wait_ge(c_ve.h, rr1)
                    c_sc.inc(s.sqrt(rr_f[:], rr_f[:]))

                blk.scalar(a_sc_rr)
                rr2 = c_sc.n

                def a_ve_rr2(v, rr2=rr2):
                    v.wait_ge(c_sc.h, rr2)
                    v.reciprocal(rr_f[:], rr_f[:])
                    _vg(v)
                    v.tensor_copy(rr_bf[:], rr_f[:])
                    c_ve.inc(v.drain())

                blk.vector(a_ve_rr2)
                rr3 = c_ve.n

                def a_tb_rrB(t, rr3=rr3):
                    t.wait_ge(c_ve.h, rr3)
                    t.matmul(rrb_ps[0][:], ones_row[:], rr_bf[0:1, 0:TH],
                             start=True, stop=True)
                    t.matmul(rrb_ps[1][:], ones_row[:], rr_bf[0:1, TH:T],
                             start=True, stop=True)
                    c_tb.inc(t.drain())

                blk.tensor(a_tb_rrB)
                rr4 = c_tb.n

                def a_ve_rrB(v, rr4=rr4):
                    v.wait_ge(c_tb.h, rr4)
                    v.tensor_copy(rrB[0][:], rrb_ps[0][:])
                    v.tensor_copy(rrB[1][:], rrb_ps[1][:])
                    _vg(v)
                    last = None
                    for kt in range(KT):
                        for th in range(2):
                            sl = slice(kt * T + th * TH, kt * T + th * TH + TH)
                            v.tensor_tensor(t1_f[:], afT[:, sl], rrB[th][:],
                                            OP.mult)
                            _vg(v)
                            v.scalar_tensor_tensor(
                                s_bf[:, sl], t1_f[:], wnr_f[:, kt:kt + 1],
                                xnT[:, sl], OP.mult, OP.add)
                    c_ve.inc(v.drain())

                blk.vector(a_ve_rrB)
                s_done = c_ve.n

                ex_tile = [ECp(nc.sbuf_tensor(f"ex_tile{i}", [128, E], F32))
                           for i in range(2)]
                eg_tile = [ECp(nc.sbuf_tensor(f"eg_tile{i}", [128, E], BF16))
                           for i in range(2)]
                es_tok = [ECp(nc.sbuf_tensor(f"es_tok{i}", [128, E], BF16))
                          for i in range(2)]
                esg = ECp(nc.sbuf_tensor("esg", [128, E], F32))
                eout = [ECp(nc.sbuf_tensor(f"eout{i}", [128, E], F32))
                        for i in range(2)]
                se_ps = [ECp(nc.psum_tensor(f"se_ps{i}", [128, KT * 128], BF16))
                         for i in range(2)]


                ex_loaded = [0] * NTP
                eg_loaded = [0] * NTP
                etr = [0] * NTP
                esv = [0] * NTP
                eov = [0] * NTP

                def e_sy0(sy):
                    for tp in range(2):
                        sy.dma_start(
                            ex_tile[tp][:],
                            bass.AP(x_d.tensor, tp * 128 * E, [[E, 128], [1, E]]),
                        ).then_inc(s_xp[tp % 2].h, 16); s_xp[tp % 2].n += 16
                        sy.dma_start(
                            eg_tile[tp][:],
                            bass.AP(g_dram.tensor, tp * 128 * E,
                                    [[E, 128], [1, E]]),
                        ).then_inc(s_xp[tp % 2].h, 16); s_xp[tp % 2].n += 16
                        eg_loaded[tp] = s_xp[tp % 2].n

                blk.sync(e_sy0)

                for tp in range(NTP):
                    def e_tb_tr(t, tp=tp):
                        if tp == 0:
                            t.wait_ge(c_ve.h, s_done)
                        if tp >= 2:
                            t.wait_ge(c_ve.h, esv[tp - 2])
                        for kt in range(KT):
                            t.transpose(
                                se_ps[tp % 2][:, kt * 128:(kt + 1) * 128],
                                s_bf[:, kt * T + tp * 128:kt * T + tp * 128 + 128],
                                ident_b[:])
                        c_tb.inc(t.drain())

                    blk.tensor(e_tb_tr)
                    etr[tp] = c_tb.n

                    def e_ve(v, tp=tp):
                        v.wait_ge(c_tb.h, etr[tp])
                        if tp >= 2:
                            v.wait_ge(s_out.h, 16 * (tp - 1))
                        v.tensor_copy(es_tok[tp % 2][:], se_ps[tp % 2][:])
                        v.wait_ge(s_xp[tp % 2].h, eg_loaded[tp])
                        _vg(v)
                        v.tensor_tensor(esg[:], es_tok[tp % 2][:],
                                        eg_tile[tp % 2][:], OP.mult)
                        _vg(v)
                        v.tensor_tensor(eout[tp % 2][:], ex_tile[tp % 2][:],
                                        esg[:], OP.add)
                        c_ve.inc(v.drain())

                    blk.vector(e_ve)
                    esv[tp] = c_ve.n
                    eov[tp] = c_ve.n

                    def e_sy(sy, tp=tp):
                        sy.wait_ge(c_ve.h, eov[tp])
                        sy.dma_start(
                            bass.AP(out_d.tensor, tp * 128 * E, [[E, 128], [1, E]]),
                            eout[tp % 2][:],
                        ).then_inc(s_out.h, 16); s_out.n += 16
                        if tp + 2 < NTP:
                            sy.dma_start(
                                ex_tile[tp % 2][:],
                                bass.AP(x_d.tensor, (tp + 2) * 128 * E,
                                        [[E, 128], [1, E]]),
                            ).then_inc(s_xp[tp % 2].h, 16); s_xp[tp % 2].n += 16
                            sy.dma_start(
                                eg_tile[tp % 2][:],
                                bass.AP(g_dram.tensor, (tp + 2) * 128 * E,
                                        [[E, 128], [1, E]]),
                            ).then_inc(s_xp[tp % 2].h, 16); s_xp[tp % 2].n += 16
                            eg_loaded[tp + 2] = s_xp[tp % 2].n

                    blk.sync(e_sy)

                def e_fin(sy):
                    sy.wait_ge(s_out.h, s_out.n)

                blk.sync(e_fin)

    return nc


_NC_CACHE = {}


def kernel(**inputs):
    x = np.asarray(inputs["L0_attn_query_input"], dtype=np.float32)
    xf = np.ascontiguousarray(x.reshape(8192, E))
    base = {
        "w_in": np.ascontiguousarray(inputs["L0_attn_W_in"], dtype=np.float32),
        "w_q": np.ascontiguousarray(inputs["L0_attn_W_q"], dtype=np.float32),
        "w_k": np.ascontiguousarray(inputs["L0_attn_W_k"], dtype=np.float32),
        "w_v": np.ascontiguousarray(inputs["L0_attn_W_v"], dtype=np.float32),
        "patterns": np.ascontiguousarray(inputs["L0_attn_stored_patterns"],
                                         dtype=np.float32),
        "w_nq": np.ascontiguousarray(inputs["L0_attn_norm_query_w"],
                                     dtype=np.float32),
        "w_nr": np.ascontiguousarray(inputs["L0_attn_norm_retrieved_w"],
                                     dtype=np.float32),
        "beta": np.asarray(inputs["L0_attn_beta"], dtype=np.float32).reshape(1, 1),
        "w_r": np.ascontiguousarray(inputs["L0_ffn_Wr"], dtype=np.float32),
        "b_r": np.asarray(inputs["L0_ffn_br"], dtype=np.float32).reshape(1, NE),
        "w_g": np.ascontiguousarray(inputs["L0_ffn_Wg"], dtype=np.float32),
        "w_u": np.ascontiguousarray(inputs["L0_ffn_Wu"], dtype=np.float32),
        "w_d": np.ascontiguousarray(inputs["L0_ffn_Wd"], dtype=np.float32),
    }
    nc = build_program()
    in_maps = []
    for c in range(8):
        m = dict(base)
        m["x"] = np.ascontiguousarray(xf[c * T:(c + 1) * T])
        in_maps.append(m)
    res = run_bass_kernel_spmd(nc, in_maps, core_ids=list(range(8)))
    out = np.concatenate([res.results[c]["out"] for c in range(8)], axis=0)
    return out.reshape(4, 2048, E)
